# version 14
# baseline (speedup 1.0000x reference)
import math
from contextlib import ExitStack

import numpy as np
import ml_dtypes

import concourse.bass as bass
import concourse.mybir as mybir
from concourse.bass_utils import run_bass_kernel_spmd

F32 = mybir.dt.float32
BF16 = mybir.dt.bfloat16
I32 = mybir.dt.int32
AF = mybir.ActivationFunctionType
ALU = mybir.AluOpType

N_CORES = 8
B_CORE = 2
SEQ = 2048
D = 1024
H = 8
HD = 128
C = 128
TG = 512
N_CH = TG // C
NMAC = (B_CORE * SEQ) // TG
FF = 3072
NHH = FF // 128
EPS = 1e-5
QSCALE = 1.0 / 127.0
MAGIC = 0x5F3759DF

_G_LO = math.log(1.0 / 512.0)
_G_HI = math.log(1.0 / 32.0)
GAMMAS = [1.0 - math.exp(_G_LO + (_G_HI - _G_LO) * i / (H - 1)) for i in range(H)]

ENGS = ("pe", "act", "dve", "gps", "sync")
SELF_SYNC = {"dve", "act", "gps"}
DEBUG = False


class Sched:

    def __init__(self):
        self.ops = []
        self.last_writer = {}
        self.readers = {}

    def add(self, eng, fn, reads=(), writes=(), dma=False, tag="", chan=None):
        idx = len(self.ops)
        deps = set()
        for r in reads:
            w = self.last_writer.get(r)
            if w is not None:
                deps.add(w)
        for w in writes:
            lw = self.last_writer.get(w)
            if lw is not None:
                deps.add(lw)
            for rd in self.readers.get(w, ()):
                deps.add(rd)
        key = chan if (dma and chan) else eng
        self.ops.append(dict(idx=idx, eng=eng, fn=fn, deps=deps, dma=dma,
                             inc=False, tag=tag, amount=16 if dma else 1,
                             key=key))
        for r in reads:
            self.readers.setdefault(r, []).append(idx)
        for w in writes:
            self.last_writer[w] = idx
            self.readers[w] = []
        return idx

    def schedule(self):
        ops = self.ops
        for op in ops:
            if op["dma"]:
                op["inc"] = True
        for op in ops:
            for d in op["deps"]:
                dep = ops[d]
                if dep["eng"] == op["eng"] and not dep["dma"] and not op["dma"]:
                    if dep["eng"] == "pe":
                        continue
                    dep["inc"] = True
                else:
                    dep["inc"] = True
        counts = {}
        for op in ops:
            if op["inc"]:
                counts[op["key"]] = counts.get(op["key"], 0) + op["amount"]
                op["sem_val"] = counts[op["key"]]
        for op in ops:
            waits = {}
            if op["dma"]:
                waits[op["key"]] = op["sem_val"] - 16
            for d in op["deps"]:
                dep = ops[d]
                if not dep["inc"]:
                    continue
                k = dep["key"]
                if k == op["key"] and not dep["dma"] and not op["dma"] \
                        and dep["eng"] == "pe":
                    continue
                v = dep["sem_val"]
                if waits.get(k, 0) < v:
                    waits[k] = v
            op["waits"] = {k: v for k, v in waits.items() if v > 0}
        self.final_counts = counts
        self.keys = sorted(counts.keys())
        per_eng = {e: [] for e in ENGS}
        for op in ops:
            per_eng[op["eng"]].append(op)
        self.per_eng = per_eng


def _ap(t, offset, dims):
    return bass.AP(t, offset, dims)


def build_program():
    nc = bass.Bass("TRN2", target_bir_lowering=False)

    x_t = nc.dram_tensor("x_t", [128, B_CORE, NMAC // 2, 8, TG], F32,
                         kind="ExternalInput")
    wq_d = nc.dram_tensor("wq", [128, H * HD], F32, kind="ExternalInput")
    wk_d = nc.dram_tensor("wk", [128, H * HD], F32, kind="ExternalInput")
    wv_d = nc.dram_tensor("wv", [128, H * HD], F32, kind="ExternalInput")
    w1_d = nc.dram_tensor("w1", [D, D], F32, kind="ExternalInput")
    w2_d = nc.dram_tensor("w2", [D, D], F32, kind="ExternalInput")
    gnw_d = nc.dram_tensor("gnwT", [128, H], F32, kind="ExternalInput")
    gnb_d = nc.dram_tensor("gnbT", [128, H], F32, kind="ExternalInput")
    wgT_d = nc.dram_tensor("wgT", [128, NHH * 1024], I32, kind="ExternalInput")
    wuT_d = nc.dram_tensor("wuT", [128, NHH * 1024], I32, kind="ExternalInput")
    wdT_d = nc.dram_tensor("wdT", [128, 8 * FF], I32, kind="ExternalInput")
    cgq_d = nc.dram_tensor("cgq", [128, D], F32, kind="ExternalInput")
    cgk_d = nc.dram_tensor("cgk", [128, D], F32, kind="ExternalInput")
    cgd_d = nc.dram_tensor("cgd", [128, H], F32, kind="ExternalInput")
    y_t = nc.dram_tensor("y_t", [128, B_CORE, NMAC // 2, 8, TG], F32,
                         kind="ExternalOutput")
    scr_gu = nc.dram_tensor("scr_gu", [128, NHH * 2048], BF16)
    scr_d = nc.dram_tensor("scr_d", [128, 8 * FF], BF16)

    dbg = {}
    if DEBUG:
        for nm, shape, dt in [
                ("dbg_h1", [128, 8 * TG], BF16), ("dbg_qs", [128, H * TG], BF16),
                ("dbg_ks", [128, H * TG], BF16), ("dbg_ret", [128, 32 * 128], BF16),
                ("dbg_u", [128, 8 * TG], BF16), ("dbg_x1", [128, 8 * TG], F32),
                ("dbg_h2", [128, 8 * TG], BF16), ("dbg_hid", [128, NHH * TG], BF16),
                ("dbg_S", [128, 16 * 128], F32), ("dbg_ssum", [128, 32], F32),
                ("dbg_rstd", [128, 32], F32), ("dbg_ktm", [128, H * 128], BF16),
                ("dbg_vtm", [128, H * 128], BF16), ("dbg_sct", [128, H * 128], BF16)]:
            dbg[nm] = nc.dram_tensor(nm, shape, dt, kind="ExternalOutput")

    S = Sched()

    with ExitStack() as st:
        e = st.enter_context

        def sb(name, shape, dt):
            return e(nc.sbuf_tensor(name, shape, dt))

        x_sb = [sb("x0", [128, 8 * TG], F32), sb("x1", [128, 8 * TG], F32)]
        hid = sb("hid", [128, NHH * TG], BF16)
        u_sb = sb("u", [128, 8 * TG], BF16)
        h_sb = [sb("h0", [128, 8 * TG], BF16), sb("h1", [128, 8 * TG], BF16)]
        qs_sb = sb("qs", [128, H * TG], BF16)
        ks_sb = sb("ks", [128, H * TG], BF16)
        ktm = sb("ktm", [128, H * 128], BF16)
        vtm = sb("vtm", [128, H * 128], BF16)
        sct = sb("sct", [128, H * 128], BF16)
        ret_sb = sb("ret", [128, 32 * 128], BF16)
        retn = sb("retn", [128, 4 * 128], BF16)
        sq_sb = sb("sq", [128, 4 * 128], BF16)
        S_f = sb("S_f", [128, 16 * 128], F32)
        S_b = sb("S_b", [128, 16 * 128], BF16)
        w1_sb = sb("w1s", [128, 8 * 1024], BF16)
        w2_sb = sb("w2s", [128, 8 * 1024], BF16)
        wq_sb = sb("wqs", [128, H * 128], BF16)
        wk_sb = sb("wks", [128, H * 128], BF16)
        wv_sb = sb("wvs", [128, H * 128], BF16)
        wgu_sb = [sb("wgu0", [128, 2048], BF16), sb("wgu1", [128, 2048], BF16)]
        wd_sb = [sb("wd0", [128, 12 * 128], BF16), sb("wd1", [128, 12 * 128], BF16)]
        cgq = sb("cgqs", [128, D], F32)
        cgk = sb("cgks", [128, D], F32)
        cgd = sb("cgds", [128, H], F32)
        gnw = sb("gnws", [128, H], F32)
        gnb = sb("gnbs", [128, H], F32)
        causal = sb("causal", [128, 128], BF16)
        ident = sb("ident", [128, 128], BF16)
        ones_c = sb("ones_c", [128, 1], BF16)
        ones_r = sb("ones_r", [1, 128], F32)
        cst_row = sb("cst_row", [1, TG], I32)
        cstC = sb("cstC", [128, 32], I32)
        rows = sb("rows", [128, 2048], F32)
        ln_scr = sb("ln_scr", [128, TG], F32)
        ssum = sb("ssum", [128, 32], F32)
        ssq = sb("ssq", [128, 32], F32)
        mu_g = sb("mu_g", [128, 32], F32)
        rstd_g = sb("rstd_g", [128, 32], F32)
        tmp_g = sb("tmp_g", [128, 32], F32)
        varpe_g = sb("varpe_g", [128, 32], F32)
        bcol_g = sb("bcol_g", [128, 32], F32)
        stgi = [hid[:, 6144:8192].bitcast(I32), hid[:, 8192:10240].bitcast(I32)]
        stgb = [hid[:, 10240:11264], hid[:, 11264:12288]]
        stgi_res = [["hid12", "hid13", "hid14", "hid15"],
                    ["hid16", "hid17", "hid18", "hid19"]]
        stgb_res = [["hid20", "hid21"], ["hid22", "hid23"]]
        sg_sb = [sb("sg0", [128, TG], BF16), sb("sg1", [128, TG], BF16)]

        pb = [e(nc.psum_tensor(f"pb{i}", [128, 512], F32)) for i in range(8)]
        PBN = [f"pb{i}" for i in range(8)]

        sems = {en: e(nc.semaphore(f"s_{en}")) for en in ENGS}

        def row(slot, dt=None):
            a = rows[0:1, slot * TG:(slot + 1) * TG]
            return a.bitcast(dt) if dt else a

        rot_ret = [5, 6, 7]
        rot_i = [0]

        def next_ret_bank():
            b = rot_ret[rot_i[0] % len(rot_ret)]
            rot_i[0] += 1
            return b

        def warm(n):
            pass

        def emit_startup():
            S.add("gps", lambda g: g.memset(ident[:], 0.0), writes=["ident"])
            S.add("gps", lambda g: g.affine_select(
                out=ident[:], in_=ident[:], compare_op=ALU.not_equal,
                fill=1.0, base=0, pattern=[[-1, 128]], channel_multiplier=1),
                reads=["ident"], writes=["ident"])
            S.add("gps", lambda g: g.memset(causal[:], 1.0), writes=["causal"])
            S.add("gps", lambda g: g.affine_select(
                out=causal[:], in_=causal[:], compare_op=ALU.is_ge,
                fill=0.0, base=0, pattern=[[1, 128]], channel_multiplier=-1),
                reads=["causal"], writes=["causal"])
            S.add("gps", lambda g: g.memset(ones_c[:], 1.0), writes=["ones_c"])
            S.add("gps", lambda g: g.memset(ones_r[:], 1.0), writes=["ones_r"])
            S.add("gps", lambda g: g.memset(cst_row[:], MAGIC), writes=["cst_row"])
            S.add("gps", lambda g: g.memset(cstC[:], MAGIC), writes=["cstC"])
            S.add("gps", lambda g: g.memset(S_f[:], 0.0), writes=[f"S{b}_{h}" for b in range(2) for h in range(H)])
            S.add("gps", lambda g: g.memset(S_b[:], 0.0), writes=[f"Sb{b}_{h}" for b in range(2) for h in range(H)])

            for nm, dst, src in [("cgq", cgq, cgq_d), ("cgk", cgk, cgk_d),
                                 ("cgd", cgd, cgd_d), ("gnw", gnw, gnw_d),
                                 ("gnb", gnb, gnb_d)]:
                S.add("sync", lambda s, dst=dst, src=src: s.dma_start(
                    out=dst[:], in_=src.ap()), writes=[nm], dma=True,
                    chan="cst")


            for nm, wsb, wd in [("wq", wq_sb, wq_d), ("wk", wk_sb, wk_d),
                                ("wv", wv_sb, wv_d)]:
                S.add("sync", lambda s, wd=wd: s.dma_start(
                    out=stgi[0].bitcast(F32), in_=wd.ap()),
                    writes=stgi_res[0], dma=True, tag=f"ld_{nm}", chan="cst")
                S.add("dve", lambda v, wsb=wsb: v.tensor_copy(
                    out=wsb[:], in_=stgi[0].bitcast(F32)),
                    reads=stgi_res[0], writes=[nm])
        def emit_startup_w12():
            wch = ["dqi0", "dqi1", "st0", "st1"]
            for wi, (nm, wsb, wd) in enumerate(
                    [("w1", w1_sb, w1_d), ("w2", w2_sb, w2_d)]):
                for dd in range(8):
                    slot = dd % 2
                    ch = wch[(wi * 8 + dd) % 4]
                    S.add("sync", lambda s, wd=wd, dd=dd, slot=slot: s.dma_start(
                        out=stgi[slot].bitcast(F32),
                        in_=_ap(wd, dd * 128 * D, [[D, 128], [1, D]])),
                        writes=stgi_res[slot], dma=True, tag=f"ld_{nm}{dd}", chan=ch)
                    S.add("dve", lambda v, wsb=wsb, dd=dd, slot=slot: v.tensor_copy(
                        out=wsb[:, dd * 1024:(dd + 1) * 1024],
                        in_=stgi[slot].bitcast(F32)),
                        reads=stgi_res[slot], writes=[f"{nm}_{dd}"])

        deq_jobs = []
        for hh in range(NHH):
            deq_jobs.append(("gu", hh, 0))
            deq_jobs.append(("gu", hh, 1))
        for ff in range(8):
            for g3 in range(3):
                deq_jobs.append(("d", ff, g3))
        deq_pos = [0]

        def emit_dequant(n):
            for _ in range(n):
                if deq_pos[0] >= len(deq_jobs):
                    return
                kind, a, b = deq_jobs[deq_pos[0]]
                k = deq_pos[0]
                deq_pos[0] += 1
                slot = k % 2
                if kind == "gu":
                    hh, w = a, b
                    src = wgT_d if w == 0 else wuT_d
                    in_ap = _ap(src, hh * 1024, [[NHH * 1024, 128], [1, 1024]])
                    out_ap = _ap(scr_gu, hh * 2048 + w * 1024,
                                 [[NHH * 2048, 128], [1, 1024]])
                    scr_res = f"scr_gu{hh}"
                    tag = f"gu{hh}_{w}"
                else:
                    ff, g3 = a, b
                    in_ap = _ap(wdT_d, ff * FF + g3 * 1024,
                                [[8 * FF, 128], [1, 1024]])
                    out_ap = _ap(scr_d, ff * FF + g3 * 1024,
                                 [[8 * FF, 128], [1, 1024]])
                    scr_res = f"scr_d{ff}_{g3 // 2}" if g3 < 2 else f"scr_d{ff}_1"
                    scr_res = None
                    tag = f"d{ff}_{g3}"
                S.add("sync", lambda s, in_ap=in_ap, slot=slot: s.dma_start(
                    out=stgi[slot], in_=in_ap),
                    writes=stgi_res[slot], dma=True, tag=f"dq_in_{tag}", chan=f"dqi{slot}")
                S.add("dve", lambda v, slot=slot: v.tensor_scalar(
                    out=stgb[slot], in0=stgi[slot], scalar1=QSCALE,
                    scalar2=None, op0=ALU.mult),
                    reads=stgi_res[slot], writes=stgb_res[slot])
                if kind == "gu":
                    wres = [f"scr_gu{a}"]
                else:
                    ff, g3 = a, b
                    wres = ([f"scr_d{ff}_0"] if g3 == 0 else
                            [f"scr_d{ff}_0", f"scr_d{ff}_1"] if g3 == 1 else
                            [f"scr_d{ff}_1"])
                S.add("gps", lambda g, out_ap=out_ap, slot=slot: g.dma_start(
                    out=out_ap, in_=stgb[slot]),
                    reads=stgb_res[slot], writes=wres,
                    dma=True, tag=f"dq_out_{tag}", chan=f"dqo{slot}")

        def emit_ln_p1(ms, hs, tag, fine=False):
            xs = x_sb[ms]
            xres = [f"x{ms}_{dd}" for dd in range(8)]
            for dd in range(8):
                S.add("dve", lambda v, dd=dd: v.tensor_copy(
                    out=u_sb[:, dd * TG:(dd + 1) * TG],
                    in_=xs[:, dd * TG:(dd + 1) * TG]),
                    reads=[xres[dd]], writes=[f"u{dd}"], tag=f"{tag}_cast{dd}")
                S.add("act", lambda sc, dd=dd: sc.activation(
                    out=ret_sb[:, dd * TG:(dd + 1) * TG],
                    in_=u_sb[:, dd * TG:(dd + 1) * TG], func=AF.Square),
                    reads=[f"u{dd}"],
                    writes=[f"ret{4 * dd + j}" for j in range(4)],
                    tag=f"{tag}_sq{dd}")
            for dd in range(8):
                S.add("pe", lambda t, dd=dd: t.matmul(
                    out=pb[4][0:1, :], lhsT=ones_c[:],
                    rhs=u_sb[:, dd * TG:(dd + 1) * TG],
                    start=(dd == 0), stop=(dd == 7)),
                    reads=["ones_c", f"u{dd}"], writes=["pb4"],
                    tag=f"{tag}_smm{dd}")
            for dd in range(8):
                S.add("pe", lambda t, dd=dd: t.matmul(
                    out=pb[2][0:1, :], lhsT=ones_c[:],
                    rhs=ret_sb[:, dd * TG:(dd + 1) * TG],
                    start=(dd == 0), stop=(dd == 7)),
                    reads=["ones_c"] + [f"ret{4 * dd + j}" for j in range(4)],
                    writes=["pb2"], tag=f"{tag}_qmm{dd}")
            if fine:
                yield
            S.add("dve", lambda v: v.tensor_copy(
                out=row(0), in_=pb[4][0:1, :]),
                reads=["pb4"], writes=["r_sa"], tag=f"{tag}_ev1")
            S.add("dve", lambda v: v.tensor_copy(
                out=row(1), in_=pb[2][0:1, :]),
                reads=["pb2"], writes=["r_sb"], tag=f"{tag}_ev2")
            S.add("dve", lambda v: v.tensor_scalar(
                out=rows[0:1, 1024:2048], in0=rows[0:1, 0:1024],
                scalar1=1.0 / D, scalar2=None, op0=ALU.mult),
                reads=["r_sa", "r_sb"], writes=["r_mu", "r_m2"],
                tag=f"{tag}_mu")
            S.add("dve", lambda v: v.tensor_tensor(
                out=row(1), in0=row(2), in1=row(2), op=ALU.mult),
                reads=["r_mu"], writes=["r_sb"], tag=f"{tag}_musq")
            S.add("dve", lambda v: v.scalar_tensor_tensor(
                out=row(3), in0=row(3), scalar=EPS, in1=row(1),
                op0=ALU.add, op1=ALU.subtract),
                reads=["r_m2", "r_sb"], writes=["r_var"], tag=f"{tag}_var")
            S.add("dve", lambda v: v.tensor_scalar(
                out=row(1, dt=I32), in0=row(3, dt=I32), scalar1=1,
                scalar2=None, op0=ALU.logical_shift_right),
                reads=["r_var"], writes=["r_sb"], tag=f"{tag}_sh")
            S.add("dve", lambda v: v.tensor_tensor(
                out=row(0, dt=I32), in0=cst_row[:], in1=row(1, dt=I32),
                op=ALU.subtract),
                reads=["cst_row", "r_sb"], writes=["r_y"], tag=f"{tag}_y0")
            for it in range(1):
                S.add("dve", lambda v: v.tensor_tensor(
                    out=row(1), in0=row(0), in1=row(0), op=ALU.mult),
                    reads=["r_y"], writes=["r_sb"], tag=f"{tag}_n{it}a")
                S.add("dve", lambda v: v.tensor_tensor(
                    out=row(1), in0=row(1), in1=row(3), op=ALU.mult),
                    reads=["r_sb", "r_var"], writes=["r_sb"], tag=f"{tag}_n{it}b")
                S.add("dve", lambda v: v.tensor_scalar(
                    out=row(1), in0=row(1), scalar1=-0.5, scalar2=1.5,
                    op0=ALU.mult, op1=ALU.add),
                    reads=["r_sb"], writes=["r_sb"], tag=f"{tag}_n{it}c")
                S.add("dve", lambda v: v.tensor_tensor(
                    out=row(0), in0=row(0), in1=row(1), op=ALU.mult),
                    reads=["r_y", "r_sb"], writes=["r_y"], tag=f"{tag}_n{it}d")
            if fine:
                yield

        def emit_ln_p2(ms, hs, tag, fine=False):
            xs = x_sb[ms]
            xres = [f"x{ms}_{dd}" for dd in range(8)]
            for c in range(N_CH):
                S.add("pe", lambda t, c=c: t.matmul(
                    out=pb[2][:, c * 128:(c + 1) * 128], lhsT=ones_r[:],
                    rhs=rows[0:1, c * 128:(c + 1) * 128],
                    start=True, stop=True),
                    reads=["ones_r", "r_y"], writes=["pb2"], tag=f"{tag}_bA{c}")
                S.add("pe", lambda t, c=c: t.matmul(
                    out=pb[3][:, c * 128:(c + 1) * 128], lhsT=ones_r[:],
                    rhs=rows[0:1, 1024 + c * 128:1024 + (c + 1) * 128],
                    start=True, stop=True),
                    reads=["ones_r", "r_mu"], writes=["pb3"], tag=f"{tag}_bB{c}")
            if fine:
                yield
            for dd in range(8):
                S.add("dve", lambda v, dd=dd: v.tensor_tensor(
                    out=ln_scr[:], in0=xs[:, dd * TG:(dd + 1) * TG],
                    in1=pb[3][:], op=ALU.subtract),
                    reads=[xres[dd], "pb3"], writes=["ln_scr"],
                    tag=f"{tag}_sub{dd}")
                S.add("dve", lambda v, dd=dd, hs=hs: v.tensor_tensor(
                    out=h_sb[hs][:, dd * TG:(dd + 1) * TG], in0=ln_scr[:],
                    in1=pb[2][:], op=ALU.mult),
                    reads=["ln_scr", "pb2"], writes=[f"h{hs}_{dd}"],
                    tag=f"{tag}_mul{dd}")
                if fine and dd in (1, 3, 5):
                    yield

        def dump(nm, src_ap, reads):
            if nm in dbg:
                S.add("gps", lambda g, nm=nm, src_ap=src_ap: g.dma_start(
                    out=dbg[nm].ap(), in_=src_ap), reads=reads, writes=[nm],
                    dma=True, tag=nm, chan="dbg")

        def xload(mm):
            mslot = mm % 2
            lb2, tg2 = mm % 2, mm // 2
            S.add("sync", lambda s, mslot=mslot, lb2=lb2, tg2=tg2: s.dma_start(
                out=x_sb[mslot][:],
                in_=_ap(x_t, (lb2 * (NMAC // 2) + tg2) * 4096,
                        [[B_CORE * (NMAC // 2) * 4096, 128], [1, 4096]])),
                writes=[f"x{mslot}_{dd}" for dd in range(8)], dma=True,
                tag=f"xload{mm}", chan=f"x{mslot}")

        def emit_mixer_parts(m):
            ms = m % 2
            hs = m % 2
            lb = m % 2
            xres = [f"x{ms}_{dd}" for dd in range(8)]
            hres = [f"h{hs}_{dd}" for dd in range(8)]
            hcur = h_sb[hs]

            xload(m)
            if m > 0:
                yield
            if m == 0:
                for _ in emit_ln_p1(ms, hs, f"ln1_{m}", fine=True):
                    pass
                for _ in emit_ln_p2(ms, hs, f"ln1_{m}", fine=True):
                    pass
                emit_startup_w12()
            else:
                yield from emit_ln_p1(ms, hs, f"ln1_{m}", fine=True)
                yield
                yield from emit_ln_p2(ms, hs, f"ln1_{m}", fine=True)
            if m == 0:
                dump("dbg_h1", hcur[:], hres)
            yield

            for h in range(H):
                bank = next_ret_bank()
                S.add("pe", lambda t, h=h, bank=bank: t.matmul(
                    out=pb[bank][:], lhsT=wq_sb[:, h * 128:(h + 1) * 128],
                    rhs=hcur[:, h * TG:(h + 1) * TG], start=True, stop=True),
                    reads=["wq", hres[h]], writes=[PBN[bank]], tag=f"qmm{m}_{h}")
                S.add("dve", lambda v, h=h, bank=bank: v.tensor_tensor(
                    out=_ap(qs_sb, h * TG, [[H * TG, 128], [128, 4], [1, 128]]),
                    in0=_ap(pb[bank], 0, [[512, 128], [128, 4], [1, 128]]),
                    in1=_ap(cgq, h * 128, [[D, 128], [0, 4], [1, 128]]),
                    op=ALU.mult),
                    reads=[PBN[bank], "cgq"], writes=[f"qs{h}"], tag=f"qev{m}_{h}")
                bank2 = next_ret_bank()
                S.add("pe", lambda t, h=h, bank2=bank2: t.matmul(
                    out=pb[bank2][:], lhsT=wk_sb[:, h * 128:(h + 1) * 128],
                    rhs=hcur[:, h * TG:(h + 1) * TG], start=True, stop=True),
                    reads=["wk", hres[h]], writes=[PBN[bank2]], tag=f"kmm{m}_{h}")
                S.add("dve", lambda v, h=h, bank2=bank2: v.tensor_tensor(
                    out=_ap(ks_sb, h * TG, [[H * TG, 128], [128, 4], [1, 128]]),
                    in0=_ap(pb[bank2], 0, [[512, 128], [128, 4], [1, 128]]),
                    in1=_ap(cgk, h * 128, [[D, 128], [0, 4], [1, 128]]),
                    op=ALU.mult),
                    reads=[PBN[bank2], "cgk"], writes=[f"ks{h}"], tag=f"kev{m}_{h}")
                if h == 3:
                    yield
            yield

            for gg in range(8):
                bank = next_ret_bank()
                for dd in range(8):
                    S.add("pe", lambda t, gg=gg, dd=dd, bank=bank: t.matmul(
                        out=pb[bank][:],
                        lhsT=w1_sb[:, dd * 1024 + gg * 128:dd * 1024 + (gg + 1) * 128],
                        rhs=hcur[:, dd * TG:(dd + 1) * TG],
                        start=(dd == 0), stop=(dd == 7)),
                        reads=[f"w1_{dd}", hres[dd]], writes=[PBN[bank]],
                        tag=f"w1mm{m}_{gg}_{dd}")
                S.add("act", lambda sc, gg=gg, bank=bank: sc.activation(
                    out=u_sb[:, gg * TG:(gg + 1) * TG], in_=pb[bank][:],
                    func=AF.Silu),
                    reads=[PBN[bank]], writes=[f"u{gg}"], tag=f"w1silu{m}_{gg}")
                S.add("dve", lambda v, gg=gg: v.tensor_scalar(
                    out=u_sb[:, gg * TG:(gg + 1) * TG],
                    in0=u_sb[:, gg * TG:(gg + 1) * TG],
                    scalar1=gnb[:, gg:gg + 1], scalar2=None, op0=ALU.add),
                    reads=[f"u{gg}", "gnb"], writes=[f"u{gg}"], tag=f"gnb{m}_{gg}")
                if gg == 3:
                    yield
            if m == 0:
                dump("dbg_qs", qs_sb[:], [f"qs{h}" for h in range(H)])
                dump("dbg_ks", ks_sb[:], [f"ks{h}" for h in range(H)])
            yield

            for c in range(N_CH):
                for h in range(H):
                    bk = next_ret_bank()
                    S.add("pe", lambda t, h=h, c=c, bk=bk: t.matmul(
                        out=pb[bk][:, 0:128],
                        lhsT=hcur[:, h * TG + c * 128:h * TG + (c + 1) * 128],
                        rhs=wk_sb[:, h * 128:(h + 1) * 128],
                        start=True, stop=True),
                        reads=[hres[h], "wk"], writes=[PBN[bk]],
                        tag=f"ktm{m}_{c}_{h}")
                    S.add("act", lambda sc, h=h, bk=bk: sc.activation(
                        out=ktm[:, h * 128:(h + 1) * 128], in_=pb[bk][:, 0:128],
                        func=AF.Identity, scale=cgd[:, h:h + 1]),
                        reads=[PBN[bk], "cgd"], writes=[f"ktm{h}"],
                        tag=f"ktmev{m}_{c}_{h}")
                    bv = next_ret_bank()
                    S.add("pe", lambda t, h=h, c=c, bv=bv: t.matmul(
                        out=pb[bv][:, 0:128],
                        lhsT=hcur[:, h * TG + c * 128:h * TG + (c + 1) * 128],
                        rhs=wv_sb[:, h * 128:(h + 1) * 128],
                        start=True, stop=True),
                        reads=[hres[h], "wv"], writes=[PBN[bv]],
                        tag=f"vtm{m}_{c}_{h}")
                    S.add("act", lambda sc, h=h, bv=bv: sc.activation(
                        out=vtm[:, h * 128:(h + 1) * 128], in_=pb[bv][:, 0:128],
                        func=AF.Copy),
                        reads=[PBN[bv]], writes=[f"vtm{h}"],
                        tag=f"vtmev{m}_{c}_{h}")
                    bs = next_ret_bank()
                    S.add("pe", lambda t, h=h, c=c, bs=bs: t.matmul(
                        out=pb[bs][:, 0:128],
                        lhsT=ks_sb[:, h * TG + c * 128:h * TG + (c + 1) * 128],
                        rhs=qs_sb[:, h * TG + c * 128:h * TG + (c + 1) * 128],
                        start=True, stop=True),
                        reads=[f"ks{h}", f"qs{h}"], writes=[PBN[bs]],
                        tag=f"sc{m}_{c}_{h}")
                    S.add("dve", lambda v, h=h, bs=bs: v.tensor_tensor(
                        out=sct[:, h * 128:(h + 1) * 128], in0=pb[bs][:, 0:128],
                        in1=causal[:], op=ALU.mult),
                        reads=[PBN[bs], "causal"], writes=[f"sct{h}"],
                        tag=f"scev{m}_{c}_{h}")
                yield
                for h in range(H):
                    sres = f"S{lb}_{h}"
                    sbres = f"Sb{lb}_{h}"
                    ridx = c * 8 + h
                    br = next_ret_bank()
                    S.add("pe", lambda t, h=h, br=br: t.matmul(
                        out=pb[br][:, 0:128], lhsT=sct[:, h * 128:(h + 1) * 128],
                        rhs=vtm[:, h * 128:(h + 1) * 128], start=True, stop=False),
                        reads=[f"sct{h}", f"vtm{h}"], writes=[PBN[br]],
                        tag=f"ri{m}_{c}_{h}")
                    S.add("pe", lambda t, h=h, c=c, br=br, lb=lb: t.matmul(
                        out=pb[br][:, 0:128],
                        lhsT=qs_sb[:, h * TG + c * 128:h * TG + (c + 1) * 128],
                        rhs=S_b[:, (lb * 8 + h) * 128:(lb * 8 + h + 1) * 128],
                        start=False, stop=True),
                        reads=[f"qs{h}", sbres], writes=[PBN[br]],
                        tag=f"rx{m}_{c}_{h}")
                    S.add("dve", lambda v, ridx=ridx, br=br: v.tensor_scalar(
                        out=ret_sb[:, ridx * 128:(ridx + 1) * 128],
                        in0=pb[br][:, 0:128], scalar1=1.0, scalar2=None,
                        op0=ALU.mult, op1=ALU.add,
                        accum_out=ssum[:, ridx:ridx + 1]),
                        reads=[PBN[br]], writes=[f"ret{ridx}", "ssum"],
                        tag=f"rev{m}_{c}_{h}")
                    S.add("dve", lambda v, ridx=ridx: v.scalar_tensor_tensor(
                        out=sq_sb[:, (ridx % 4) * 128:(ridx % 4 + 1) * 128],
                        in0=ret_sb[:, ridx * 128:(ridx + 1) * 128], scalar=1.0,
                        in1=ret_sb[:, ridx * 128:(ridx + 1) * 128],
                        op0=ALU.mult, op1=ALU.mult,
                        accum_out=ssq[:, ridx:ridx + 1]),
                        reads=[f"ret{ridx}"], writes=[f"sqs{ridx % 4}", "ssq"],
                        tag=f"rsq{m}_{c}_{h}")
                    bkv = next_ret_bank()
                    S.add("pe", lambda t, h=h, bkv=bkv: t.matmul(
                        out=pb[bkv][:, 0:128], lhsT=ktm[:, h * 128:(h + 1) * 128],
                        rhs=vtm[:, h * 128:(h + 1) * 128], start=True, stop=True),
                        reads=[f"ktm{h}", f"vtm{h}"], writes=[PBN[bkv]],
                        tag=f"kv{m}_{c}_{h}")
                    gC = float(GAMMAS[h] ** C)
                    S.add("dve", lambda v, h=h, bkv=bkv, gC=gC, lb=lb:
                          v.scalar_tensor_tensor(
                        out=S_f[:, (lb * 8 + h) * 128:(lb * 8 + h + 1) * 128],
                        in0=S_f[:, (lb * 8 + h) * 128:(lb * 8 + h + 1) * 128],
                        scalar=gC, in1=pb[bkv][:, 0:128],
                        op0=ALU.mult, op1=ALU.add),
                        reads=[sres, PBN[bkv]], writes=[sres],
                        tag=f"st{m}_{c}_{h}")
                    S.add("act", lambda sc, h=h, lb=lb: sc.activation(
                        out=S_b[:, (lb * 8 + h) * 128:(lb * 8 + h + 1) * 128],
                        in_=S_f[:, (lb * 8 + h) * 128:(lb * 8 + h + 1) * 128],
                        func=AF.Copy),
                        reads=[sres], writes=[sbres], tag=f"stb{m}_{c}_{h}")
                    if m == 0 and h % 2 == 0:
                        emit_dequant(1)
                if m == 0 and c == 0:
                    dump("dbg_ktm", ktm[:], [f"ktm{h}" for h in range(H)])
                    dump("dbg_vtm", vtm[:], [f"vtm{h}" for h in range(H)])
                    dump("dbg_sct", sct[:], [f"sct{h}" for h in range(H)])
                yield

            if m == 0:
                dump("dbg_ret", ret_sb[:], [f"ret{i}" for i in range(32)])
                dump("dbg_S", S_f[:], [f"S{0}_{h}" for h in range(H)])
                dump("dbg_ssum", ssum[:], ["ssum"])

            S.add("dve", lambda v: v.tensor_scalar(
                out=mu_g[:], in0=ssum[:], scalar1=1.0 / HD, scalar2=None,
                op0=ALU.mult), reads=["ssum"], writes=["mu_g"], tag=f"g1_{m}")
            S.add("dve", lambda v: v.tensor_scalar(
                out=tmp_g[:], in0=ssq[:], scalar1=1.0 / HD, scalar2=None,
                op0=ALU.mult), reads=["ssq"], writes=["tmp_g"], tag=f"g2_{m}")
            S.add("dve", lambda v: v.tensor_tensor(
                out=rstd_g[:], in0=mu_g[:], in1=mu_g[:], op=ALU.mult),
                reads=["mu_g"], writes=["rstd_g"], tag=f"g3_{m}")
            S.add("dve", lambda v: v.scalar_tensor_tensor(
                out=varpe_g[:], in0=tmp_g[:], scalar=EPS, in1=rstd_g[:],
                op0=ALU.add, op1=ALU.subtract),
                reads=["tmp_g", "rstd_g"], writes=["varpe_g"], tag=f"g4_{m}")
            S.add("dve", lambda v: v.tensor_scalar(
                out=tmp_g[:].bitcast(I32), in0=varpe_g[:].bitcast(I32),
                scalar1=1, scalar2=None, op0=ALU.logical_shift_right),
                reads=["varpe_g"], writes=["tmp_g"], tag=f"g5_{m}")
            S.add("dve", lambda v: v.tensor_tensor(
                out=rstd_g[:].bitcast(I32), in0=cstC[:], in1=tmp_g[:].bitcast(I32),
                op=ALU.subtract),
                reads=["cstC", "tmp_g"], writes=["rstd_g"], tag=f"g6_{m}")
            for it in range(2):
                S.add("dve", lambda v: v.tensor_tensor(
                    out=tmp_g[:], in0=rstd_g[:], in1=rstd_g[:], op=ALU.mult),
                    reads=["rstd_g"], writes=["tmp_g"], tag=f"g7{it}_{m}")
                S.add("dve", lambda v: v.tensor_tensor(
                    out=tmp_g[:], in0=tmp_g[:], in1=varpe_g[:], op=ALU.mult),
                    reads=["tmp_g", "varpe_g"], writes=["tmp_g"], tag=f"g8{it}_{m}")
                S.add("dve", lambda v: v.tensor_scalar(
                    out=tmp_g[:], in0=tmp_g[:], scalar1=-0.5, scalar2=1.5,
                    op0=ALU.mult, op1=ALU.add),
                    reads=["tmp_g"], writes=["tmp_g"], tag=f"g9{it}_{m}")
                S.add("dve", lambda v: v.tensor_tensor(
                    out=rstd_g[:], in0=rstd_g[:], in1=tmp_g[:], op=ALU.mult),
                    reads=["rstd_g", "tmp_g"], writes=["rstd_g"], tag=f"gA{it}_{m}")
            S.add("dve", lambda v: v.scalar_tensor_tensor(
                out=bcol_g[:], in0=mu_g[:], scalar=-1.0, in1=rstd_g[:],
                op0=ALU.mult, op1=ALU.mult),
                reads=["mu_g", "rstd_g"], writes=["bcol_g"], tag=f"gB_{m}")
            yield
            for ridx in range(32):
                c, h = ridx // 8, ridx % 8
                rs = ridx % 4
                S.add("act", lambda sc, ridx=ridx, rs=rs: sc.activation(
                    out=retn[:, rs * 128:(rs + 1) * 128],
                    in_=ret_sb[:, ridx * 128:(ridx + 1) * 128],
                    func=AF.Identity, scale=rstd_g[:, ridx:ridx + 1],
                    bias=bcol_g[:, ridx:ridx + 1]),
                    reads=[f"ret{ridx}", "rstd_g", "bcol_g"],
                    writes=[f"retn{rs}"], tag=f"ap{m}_{ridx}")
                tb = 2 + (ridx % 2)
                S.add("pe", lambda t, rs=rs, tb=tb: t.transpose(
                    out=pb[tb][:].bitcast(BF16)[:, 0:128],
                    in_=retn[:, rs * 128:(rs + 1) * 128], identity=ident[:]),
                    reads=[f"retn{rs}", "ident"], writes=[PBN[tb]],
                    tag=f"tr{m}_{ridx}")
                S.add("dve", lambda v, h=h, c=c, tb=tb: v.scalar_tensor_tensor(
                    out=u_sb[:, h * TG + c * 128:h * TG + (c + 1) * 128],
                    in0=pb[tb][:].bitcast(BF16)[:, 0:128],
                    scalar=gnw[:, h:h + 1],
                    in1=u_sb[:, h * TG + c * 128:h * TG + (c + 1) * 128],
                    op0=ALU.mult, op1=ALU.add),
                    reads=[PBN[tb], "gnw", f"u{h}"], writes=[f"u{h}"],
                    tag=f"ua{m}_{ridx}")
                if ridx == 15:
                    yield
            if m == 0:
                dump("dbg_rstd", rstd_g[:], ["rstd_g"])
                dump("dbg_u", u_sb[:], [f"u{gg}" for gg in range(8)])
            yield

            for ff in range(8):
                bank = next_ret_bank()
                for gg in range(8):
                    S.add("pe", lambda t, ff=ff, gg=gg, bank=bank: t.matmul(
                        out=pb[bank][:],
                        lhsT=w2_sb[:, gg * 1024 + ff * 128:gg * 1024 + (ff + 1) * 128],
                        rhs=u_sb[:, gg * TG:(gg + 1) * TG],
                        start=(gg == 0), stop=(gg == 7)),
                        reads=[f"w2_{gg}", f"u{gg}"], writes=[PBN[bank]],
                        tag=f"w2mm{m}_{ff}_{gg}")
                S.add("dve", lambda v, ff=ff, bank=bank, ms=ms:
                      v.scalar_tensor_tensor(
                    out=x_sb[ms][:, ff * TG:(ff + 1) * TG],
                    in0=x_sb[ms][:, ff * TG:(ff + 1) * TG], scalar=1.0,
                    in1=pb[bank][:], op0=ALU.mult, op1=ALU.add),
                    reads=[xres[ff], PBN[bank]], writes=[xres[ff]],
                    tag=f"w2ev{m}_{ff}")
                if ff == 3:
                    yield
            if m == 0:
                dump("dbg_x1", x_sb[0][:], xres)

        def emit_ln2_parts(m):
            ms = m % 2
            hs = m % 2
            yield from emit_ln_p1(ms, hs, f"ln2_{m}", fine=True)
            yield
            yield from emit_ln_p2(ms, hs, f"ln2_{m}", fine=True)
            if m == 0:
                dump("dbg_h2", h_sb[hs][:], [f"h{hs}_{dd}" for dd in range(8)])

        def gu_dma(m, hh, slot):
            S.add("sync", lambda s, hh=hh, slot=slot: s.dma_start(
                out=wgu_sb[slot][:],
                in_=_ap(scr_gu, hh * 2048, [[NHH * 2048, 128], [1, 2048]])),
                reads=[f"scr_gu{hh}"], writes=[f"wgu{slot}"], dma=True,
                tag=f"lgu{m}_{hh}", chan=f"wgu{slot}")

        def gu_mm(m, hcur, hres, hh, dd, slot, which, bank):
            off = 1024 if which else 0
            S.add("pe", lambda t, hh=hh, dd=dd, slot=slot, off=off,
                  bank=bank, hcur=hcur: t.matmul(
                out=pb[bank][:],
                lhsT=wgu_sb[slot][:, off + dd * 128:off + (dd + 1) * 128],
                rhs=hcur[:, dd * TG:(dd + 1) * TG],
                start=(dd == 0), stop=(dd == 7)),
                reads=[f"wgu{slot}", hres[dd]], writes=[PBN[bank]],
                tag=f"{'u' if which else 'g'}mm{m}_{hh}_{dd}")

        def gu_tail(m, hh, slot, gbank, ubank):
            S.add("act", lambda sc, slot=slot, gbank=gbank: sc.activation(
                out=sg_sb[slot][:], in_=pb[gbank][:], func=AF.Silu),
                reads=[PBN[gbank]], writes=[f"sg{slot}"],
                tag=f"gsilu{m}_{hh}")
            S.add("dve", lambda v, hh=hh, slot=slot, ubank=ubank:
                  v.tensor_tensor(
                out=hid[:, hh * TG:(hh + 1) * TG], in0=pb[ubank][:],
                in1=sg_sb[slot][:], op=ALU.mult),
                reads=[PBN[ubank], f"sg{slot}"], writes=[f"hid{hh}"],
                tag=f"hid{m}_{hh}")

        def emit_ffn_and_next(m, nxt_gen):
            ms = m % 2
            hs = m % 2
            lb = m % 2
            tgi = m // 2
            xres = [f"x{ms}_{dd}" for dd in range(8)]
            hres = [f"h{hs}_{dd}" for dd in range(8)]
            hcur = h_sb[hs]

            def pump():
                if nxt_gen is not None:
                    try:
                        next(nxt_gen)
                    except StopIteration:
                        pass

            for hh in range(NHH):
                if m == 0 and hh < 12:
                    emit_dequant(5)
                if hh >= 2:
                    pump()
                slot = hh % 2
                gu_dma(m, hh, slot)
                for dd in range(8):
                    gu_mm(m, hcur, hres, hh, dd, slot, 0, 0)
                for dd in range(8):
                    gu_mm(m, hcur, hres, hh, dd, slot, 1, 1)
                gu_tail(m, hh, slot, 0, 1)
            if nxt_gen is not None:
                for _ in nxt_gen:
                    pass
            if m == 0:
                dump("dbg_hid", hid[:], [f"hid{t}" for t in range(NHH)])

            l2 = emit_ln2_parts(m + 1) if m + 1 < NMAC else None

            def pump2():
                if l2 is not None:
                    try:
                        next(l2)
                    except StopIteration:
                        pass

            for ff in range(8):
                pump2()
                bank = ff % 2
                for g2 in range(2):
                    slot = (ff * 2 + g2) % 2
                    S.add("sync", lambda s, ff=ff, g2=g2, slot=slot: s.dma_start(
                        out=wd_sb[slot][:],
                        in_=_ap(scr_d, ff * FF + g2 * 1536,
                                [[8 * FF, 128], [1, 1536]])),
                        reads=[f"scr_d{ff}_{g2}"], writes=[f"wds{slot}"],
                        dma=True, tag=f"lwd{m}_{ff}_{g2}", chan=f"wd{slot}")
                    for hh in range(g2 * 12, g2 * 12 + 12):
                        S.add("pe", lambda t, ff=ff, hh=hh, g2=g2, slot=slot,
                              bank=bank: t.matmul(
                            out=pb[bank][:],
                            lhsT=wd_sb[slot][:, (hh - g2 * 12) * 128:
                                             (hh - g2 * 12 + 1) * 128],
                            rhs=hid[:, hh * TG:(hh + 1) * TG],
                            start=(hh == 0), stop=(hh == NHH - 1)),
                            reads=[f"wds{slot}", f"hid{hh}"],
                            writes=[PBN[bank]], tag=f"dmm{m}_{ff}_{hh}")
                S.add("dve", lambda v, ff=ff, bank=bank, ms=ms: v.tensor_tensor(
                    out=x_sb[ms][:, ff * TG:(ff + 1) * TG],
                    in0=x_sb[ms][:, ff * TG:(ff + 1) * TG],
                    in1=pb[bank][:], op=ALU.add),
                    reads=[xres[ff], PBN[bank]], writes=[xres[ff]],
                    tag=f"dev{m}_{ff}")
            if l2 is not None:
                for _ in l2:
                    pass

            S.add("gps", lambda g, ms=ms, lb=lb, tgi=tgi: g.dma_start(
                out=_ap(y_t, (lb * (NMAC // 2) + tgi) * 4096,
                        [[B_CORE * (NMAC // 2) * 4096, 128], [1, 4096]]),
                in_=x_sb[ms][:]),
                reads=xres, writes=[f"yout{m}"], dma=True, tag=f"store{m}",
                chan=f"st{ms}")

        emit_startup()
        for _ in emit_mixer_parts(0):
            pass
        for _ in emit_ln2_parts(0):
            pass
        for m in range(NMAC):
            nxt = emit_mixer_parts(m + 1) if m + 1 < NMAC else None
            emit_ffn_and_next(m, nxt)
        emit_dequant(1000)

        S.schedule()

        sems.update({k: e(nc.semaphore(f"s_{k}"))
                     for k in S.keys if k not in sems})

        block = e(nc.Block())

        def replay(eng_name):
            def go(eng):
                hw = {}
                for op in S.per_eng[eng_name]:
                    for en, v in sorted(op["waits"].items()):
                        if hw.get(en, 0) < v:
                            eng.wait_ge(sems[en], v)
                            hw[en] = v
                    inst = op["fn"](eng)
                    if op["inc"]:
                        inst.then_inc(sems[op["key"]], op["amount"])
                if eng_name == "gps":
                    for k in ("st0", "st1"):
                        if k in S.final_counts:
                            eng.wait_ge(sems[k], S.final_counts[k])
            return go

        block.tensor(replay("pe"))
        block.scalar(replay("act"))
        block.vector(replay("dve"))
        block.gpsimd(replay("gps"))
        block.sync(replay("sync"))

    return nc


_NC_CACHE = []
_LAST = {}


def _get_nc():
    if not _NC_CACHE:
        _NC_CACHE.append(build_program())
    return _NC_CACHE[0]


def _host_constants():
    g = np.array(GAMMAS, dtype=np.float64)
    t = np.arange(128, dtype=np.float64)
    cgq = np.empty((128, D), dtype=np.float32)
    cgk = np.empty((128, D), dtype=np.float32)
    for h in range(H):
        cgq[:, h * 128:(h + 1) * 128] = (g[h] ** (t + 1.0)).astype(np.float32)[None, :]
        cgk[:, h * 128:(h + 1) * 128] = (g[h] ** (-(t + 1.0))).astype(np.float32)[None, :]
    cgd = np.empty((128, H), dtype=np.float32)
    for h in range(H):
        cgd[:, h] = (g[h] ** (127.0 - t)).astype(np.float32)
    return cgq, cgk, cgd


def kernel(L0_attn_x, L0_attn_Wq, L0_attn_Wk, L0_attn_Wv, L0_attn_W1,
           L0_attn_W2, L0_attn_gn_w, L0_attn_gn_b, L0_ffn_w_gate,
           L0_ffn_w_up, L0_ffn_w_down):
    x = np.asarray(L0_attn_x, dtype=np.float32)
    wq = np.ascontiguousarray(
        np.asarray(L0_attn_Wq, dtype=np.float32).transpose(1, 0, 2)
        .reshape(128, H * HD))
    wk = np.ascontiguousarray(
        np.asarray(L0_attn_Wk, dtype=np.float32).transpose(1, 0, 2)
        .reshape(128, H * HD))
    wv = np.ascontiguousarray(
        np.asarray(L0_attn_Wv, dtype=np.float32).transpose(1, 0, 2)
        .reshape(128, H * HD))
    w1 = np.ascontiguousarray(np.asarray(L0_attn_W1, dtype=np.float32))
    w2 = np.ascontiguousarray(np.asarray(L0_attn_W2, dtype=np.float32))
    gnwT = np.ascontiguousarray(
        np.asarray(L0_attn_gn_w, dtype=np.float32).reshape(H, 128).T)
    gnbT = np.ascontiguousarray(
        np.asarray(L0_attn_gn_b, dtype=np.float32).reshape(H, 128).T)
    wgT = np.ascontiguousarray(
        np.asarray(L0_ffn_w_gate, dtype=np.int32)
        .reshape(NHH, 128, 8, 128).transpose(3, 0, 2, 1)
        .reshape(128, NHH * 1024))
    wuT = np.ascontiguousarray(
        np.asarray(L0_ffn_w_up, dtype=np.int32)
        .reshape(NHH, 128, 8, 128).transpose(3, 0, 2, 1)
        .reshape(128, NHH * 1024))
    wdT = np.ascontiguousarray(
        np.asarray(L0_ffn_w_down, dtype=np.int32)
        .reshape(8, 128, NHH, 128).transpose(3, 0, 2, 1)
        .reshape(128, 8 * FF))
    cgq, cgk, cgd = _host_constants()

    nc = _get_nc()
    in_maps = []
    for ci in range(N_CORES):
        xs = np.ascontiguousarray(
            x[ci * B_CORE:(ci + 1) * B_CORE]
            .reshape(B_CORE, NMAC // 2, TG, 8, 128)
            .transpose(4, 0, 1, 3, 2))
        in_maps.append({
            "x_t": xs, "wq": wq, "wk": wk, "wv": wv, "w1": w1, "w2": w2,
            "gnwT": gnwT, "gnbT": gnbT, "wgT": wgT, "wuT": wuT, "wdT": wdT,
            "cgq": cgq, "cgk": cgk, "cgd": cgd,
        })
    res = run_bass_kernel_spmd(nc, in_maps, core_ids=list(range(N_CORES)))
    _LAST.clear()
    _LAST["res"] = res
    out = np.empty((16, SEQ, D), dtype=np.float32)
    for ci in range(N_CORES):
        yt = res.results[ci]["y_t"]
        out[ci * B_CORE:(ci + 1) * B_CORE] = (
            yt.transpose(1, 2, 4, 3, 0).reshape(B_CORE, SEQ, D))
    return out


if __name__ == "__main__":
    import json
    nc = build_program()
    print("built ok, instructions:", nc.n_instructions())


# version 16
# speedup vs baseline: 1.1684x; 1.1684x over previous
import math
from contextlib import ExitStack

import numpy as np
import ml_dtypes

import concourse.bass as bass
import concourse.mybir as mybir
from concourse.bass_utils import run_bass_kernel_spmd

F32 = mybir.dt.float32
BF16 = mybir.dt.bfloat16
I32 = mybir.dt.int32
AF = mybir.ActivationFunctionType
ALU = mybir.AluOpType

N_CORES = 8
B_CORE = 2
SEQ = 2048
D = 1024
H = 8
HD = 128
C = 128
TG = 512
N_CH = TG // C
NMAC = (B_CORE * SEQ) // TG
FF = 3072
NHH = FF // 128
EPS = 1e-5
QSCALE = 1.0 / 127.0
MAGIC = 0x5F3759DF

_G_LO = math.log(1.0 / 512.0)
_G_HI = math.log(1.0 / 32.0)
GAMMAS = [1.0 - math.exp(_G_LO + (_G_HI - _G_LO) * i / (H - 1)) for i in range(H)]

ENGS = ("pe", "act", "dve", "gps", "sync")
SELF_SYNC = {"dve", "act", "gps"}
DEBUG = False


class Sched:

    def __init__(self):
        self.ops = []
        self.last_writer = {}
        self.readers = {}

    def add(self, eng, fn, reads=(), writes=(), dma=False, tag="", chan=None):
        idx = len(self.ops)
        deps = set()
        for r in reads:
            w = self.last_writer.get(r)
            if w is not None:
                deps.add(w)
        for w in writes:
            lw = self.last_writer.get(w)
            if lw is not None:
                deps.add(lw)
            for rd in self.readers.get(w, ()):
                deps.add(rd)
        key = chan if (dma and chan) else eng
        self.ops.append(dict(idx=idx, eng=eng, fn=fn, deps=deps, dma=dma,
                             inc=False, tag=tag, amount=16 if dma else 1,
                             key=key))
        for r in reads:
            self.readers.setdefault(r, []).append(idx)
        for w in writes:
            self.last_writer[w] = idx
            self.readers[w] = []
        return idx

    def schedule(self):
        ops = self.ops
        for op in ops:
            if op["dma"]:
                op["inc"] = True
        for op in ops:
            for d in op["deps"]:
                dep = ops[d]
                if dep["eng"] == op["eng"] and not dep["dma"] and not op["dma"]:
                    if dep["eng"] == "pe":
                        continue
                    dep["inc"] = True
                else:
                    dep["inc"] = True
        counts = {}
        for op in ops:
            if op["inc"]:
                counts[op["key"]] = counts.get(op["key"], 0) + op["amount"]
                op["sem_val"] = counts[op["key"]]
        for op in ops:
            waits = {}
            if op["dma"]:
                waits[op["key"]] = op["sem_val"] - 16
            for d in op["deps"]:
                dep = ops[d]
                if not dep["inc"]:
                    continue
                k = dep["key"]
                if k == op["key"] and not dep["dma"] and not op["dma"] \
                        and dep["eng"] == "pe":
                    continue
                v = dep["sem_val"]
                if waits.get(k, 0) < v:
                    waits[k] = v
            op["waits"] = {k: v for k, v in waits.items() if v > 0}
        self.final_counts = counts
        self.keys = sorted(counts.keys())
        per_eng = {e: [] for e in ENGS}
        for op in ops:
            per_eng[op["eng"]].append(op)
        self.per_eng = per_eng


def _ap(t, offset, dims):
    return bass.AP(t, offset, dims)


def build_program():
    nc = bass.Bass("TRN2", target_bir_lowering=False)

    x_t = nc.dram_tensor("x_t", [128, B_CORE, NMAC // 2, 8, TG], F32,
                         kind="ExternalInput")
    wq_d = nc.dram_tensor("wq", [128, H * HD], F32, kind="ExternalInput")
    wk_d = nc.dram_tensor("wk", [128, H * HD], F32, kind="ExternalInput")
    wv_d = nc.dram_tensor("wv", [128, H * HD], F32, kind="ExternalInput")
    w1_d = nc.dram_tensor("w1", [D, D], F32, kind="ExternalInput")
    w2_d = nc.dram_tensor("w2", [D, D], F32, kind="ExternalInput")
    gnw_d = nc.dram_tensor("gnwT", [128, H], F32, kind="ExternalInput")
    gnb_d = nc.dram_tensor("gnbT", [128, H], F32, kind="ExternalInput")
    wgT_d = nc.dram_tensor("wgT", [128, NHH * 1024], I32, kind="ExternalInput")
    wuT_d = nc.dram_tensor("wuT", [128, NHH * 1024], I32, kind="ExternalInput")
    wdT_d = nc.dram_tensor("wdT", [128, 8 * FF], I32, kind="ExternalInput")
    cgq_d = nc.dram_tensor("cgq", [128, D], F32, kind="ExternalInput")
    cgk_d = nc.dram_tensor("cgk", [128, D], F32, kind="ExternalInput")
    cgd_d = nc.dram_tensor("cgd", [128, H], F32, kind="ExternalInput")
    y_t = nc.dram_tensor("y_t", [128, B_CORE, NMAC // 2, 8, TG], F32,
                         kind="ExternalOutput")
    scr_gu = nc.dram_tensor("scr_gu", [128, NHH * 2048], BF16)
    scr_d = nc.dram_tensor("scr_d", [128, 8 * FF], BF16)

    dbg = {}
    if DEBUG:
        for nm, shape, dt in [
                ("dbg_h1", [128, 8 * TG], BF16), ("dbg_qs", [128, H * TG], BF16),
                ("dbg_ks", [128, H * TG], BF16), ("dbg_ret", [128, 32 * 128], BF16),
                ("dbg_u", [128, 8 * TG], BF16), ("dbg_x1", [128, 8 * TG], F32),
                ("dbg_h2", [128, 8 * TG], BF16), ("dbg_hid", [128, NHH * TG], BF16),
                ("dbg_S", [128, 16 * 128], F32), ("dbg_ssum", [128, 32], F32),
                ("dbg_rstd", [128, 32], F32), ("dbg_ktm", [128, H * 128], BF16),
                ("dbg_vtm", [128, H * 128], BF16), ("dbg_sct", [128, H * 128], BF16)]:
            dbg[nm] = nc.dram_tensor(nm, shape, dt, kind="ExternalOutput")

    S = Sched()

    with ExitStack() as st:
        e = st.enter_context

        def sb(name, shape, dt):
            return e(nc.sbuf_tensor(name, shape, dt))

        x_sb = [sb("x0", [128, 8 * TG], F32), sb("x1", [128, 8 * TG], F32)]
        hid = sb("hid", [128, NHH * TG], BF16)
        u_sb = sb("u", [128, 8 * TG], BF16)
        h_sb = [sb("h0", [128, 8 * TG], BF16), sb("h1", [128, 8 * TG], BF16)]
        qs_sb = sb("qs", [128, H * TG], BF16)
        ks_sb = sb("ks", [128, H * TG], BF16)
        ktm = sb("ktm", [128, H * 128], BF16)
        vtm = sb("vtm", [128, H * 128], BF16)
        sct = sb("sct", [128, H * 128], BF16)
        ret_sb = sb("ret", [128, 32 * 128], BF16)
        retn = sb("retn", [128, 4 * 128], BF16)
        sq_sb = sb("sq", [128, 4 * 128], BF16)
        S_f = sb("S_f", [128, 16 * 128], F32)
        S_b = sb("S_b", [128, 16 * 128], BF16)
        w1_sb = sb("w1s", [128, 8 * 1024], BF16)
        w2_sb = sb("w2s", [128, 8 * 1024], BF16)
        wq_sb = sb("wqs", [128, H * 128], BF16)
        wk_sb = sb("wks", [128, H * 128], BF16)
        wv_sb = sb("wvs", [128, H * 128], BF16)
        wgu_sb = [sb("wgu0", [128, 2048], BF16), sb("wgu1", [128, 2048], BF16)]
        wd_sb = [sb("wd0", [128, 12 * 128], BF16), sb("wd1", [128, 12 * 128], BF16)]
        cgq = sb("cgqs", [128, D], F32)
        cgk = sb("cgks", [128, D], F32)
        cgd = sb("cgds", [128, H], F32)
        gnw = sb("gnws", [128, H], F32)
        gnb = sb("gnbs", [128, H], F32)
        causal = sb("causal", [128, 128], BF16)
        ident = sb("ident", [128, 128], BF16)
        ones_c = sb("ones_c", [128, 1], BF16)
        ones_r = sb("ones_r", [1, 128], F32)
        cst_row = sb("cst_row", [1, TG], I32)
        cstC = sb("cstC", [128, 32], I32)
        rows = sb("rows", [128, 2048], F32)
        ln_scr = sb("ln_scr", [128, TG], F32)
        ssum = sb("ssum", [128, 32], F32)
        ssq = sb("ssq", [128, 32], F32)
        mu_g = sb("mu_g", [128, 32], F32)
        rstd_g = sb("rstd_g", [128, 32], F32)
        tmp_g = sb("tmp_g", [128, 32], F32)
        varpe_g = sb("varpe_g", [128, 32], F32)
        bcol_g = sb("bcol_g", [128, 32], F32)
        stgi = [hid[:, 6144:8192].bitcast(I32), hid[:, 8192:10240].bitcast(I32)]
        stgb = [hid[:, 10240:11264], hid[:, 11264:12288]]
        stgi_res = [["hid12", "hid13", "hid14", "hid15"],
                    ["hid16", "hid17", "hid18", "hid19"]]
        stgb_res = [["hid20", "hid21"], ["hid22", "hid23"]]
        sg_sb = [sb("sg0", [128, TG], BF16), sb("sg1", [128, TG], BF16)]

        pb = [e(nc.psum_tensor(f"pb{i}", [128, 512], F32)) for i in range(8)]
        PBN = [f"pb{i}" for i in range(8)]

        sems = {en: e(nc.semaphore(f"s_{en}")) for en in ENGS}

        def row(slot, dt=None):
            a = rows[0:1, slot * TG:(slot + 1) * TG]
            return a.bitcast(dt) if dt else a

        rot_ret = [5, 6, 7]
        rot_i = [0]

        def next_ret_bank():
            b = rot_ret[rot_i[0] % len(rot_ret)]
            rot_i[0] += 1
            return b

        def warm(n):
            pass

        def emit_startup():
            S.add("gps", lambda g: g.memset(ident[:], 0.0), writes=["ident"])
            S.add("gps", lambda g: g.affine_select(
                out=ident[:], in_=ident[:], compare_op=ALU.not_equal,
                fill=1.0, base=0, pattern=[[-1, 128]], channel_multiplier=1),
                reads=["ident"], writes=["ident"])
            S.add("gps", lambda g: g.memset(causal[:], 1.0), writes=["causal"])
            S.add("gps", lambda g: g.affine_select(
                out=causal[:], in_=causal[:], compare_op=ALU.is_ge,
                fill=0.0, base=0, pattern=[[1, 128]], channel_multiplier=-1),
                reads=["causal"], writes=["causal"])
            S.add("gps", lambda g: g.memset(ones_c[:], 1.0), writes=["ones_c"])
            S.add("gps", lambda g: g.memset(ones_r[:], 1.0), writes=["ones_r"])
            S.add("gps", lambda g: g.memset(cst_row[:], MAGIC), writes=["cst_row"])
            S.add("gps", lambda g: g.memset(cstC[:], MAGIC), writes=["cstC"])
            S.add("gps", lambda g: g.memset(S_f[:], 0.0), writes=[f"S{b}_{h}" for b in range(2) for h in range(H)])
            S.add("gps", lambda g: g.memset(S_b[:], 0.0), writes=[f"Sb{b}_{h}" for b in range(2) for h in range(H)])

            for nm, dst, src in [("cgq", cgq, cgq_d), ("cgk", cgk, cgk_d),
                                 ("cgd", cgd, cgd_d), ("gnw", gnw, gnw_d),
                                 ("gnb", gnb, gnb_d)]:
                S.add("sync", lambda s, dst=dst, src=src: s.dma_start(
                    out=dst[:], in_=src.ap()), writes=[nm], dma=True,
                    chan="cst")


            for nm, wsb, wd in [("wq", wq_sb, wq_d), ("wk", wk_sb, wk_d),
                                ("wv", wv_sb, wv_d)]:
                S.add("sync", lambda s, wd=wd: s.dma_start(
                    out=stgi[0].bitcast(F32), in_=wd.ap()),
                    writes=stgi_res[0], dma=True, tag=f"ld_{nm}", chan="cst")
                S.add("dve", lambda v, wsb=wsb: v.tensor_copy(
                    out=wsb[:], in_=stgi[0].bitcast(F32)),
                    reads=stgi_res[0], writes=[nm])
        def emit_startup_w12():
            wch = ["dqi0", "dqi1", "wl0", "wl1"]
            for wi, (nm, wsb, wd) in enumerate(
                    [("w1", w1_sb, w1_d), ("w2", w2_sb, w2_d)]):
                for dd in range(8):
                    slot = dd % 2
                    ch = wch[(wi * 8 + dd) % 4]
                    S.add("sync", lambda s, wd=wd, dd=dd, slot=slot: s.dma_start(
                        out=stgi[slot].bitcast(F32),
                        in_=_ap(wd, dd * 128 * D, [[D, 128], [1, D]])),
                        writes=stgi_res[slot], dma=True, tag=f"ld_{nm}{dd}", chan=ch)
                    S.add("dve", lambda v, wsb=wsb, dd=dd, slot=slot: v.tensor_copy(
                        out=wsb[:, dd * 1024:(dd + 1) * 1024],
                        in_=stgi[slot].bitcast(F32)),
                        reads=stgi_res[slot], writes=[f"{nm}_{dd}"])

        deq_jobs = []
        for hh in range(NHH):
            deq_jobs.append(("gu", hh, 0))
            deq_jobs.append(("gu", hh, 1))
        for ff in range(8):
            for g3 in range(3):
                deq_jobs.append(("d", ff, g3))
        deq_pos = [0]

        def emit_dequant(n):
            for _ in range(n):
                if deq_pos[0] >= len(deq_jobs):
                    return
                kind, a, b = deq_jobs[deq_pos[0]]
                k = deq_pos[0]
                deq_pos[0] += 1
                slot = k % 2
                if kind == "gu":
                    hh, w = a, b
                    src = wgT_d if w == 0 else wuT_d
                    in_ap = _ap(src, hh * 1024, [[NHH * 1024, 128], [1, 1024]])
                    out_ap = _ap(scr_gu, hh * 2048 + w * 1024,
                                 [[NHH * 2048, 128], [1, 1024]])
                    scr_res = f"scr_gu{hh}"
                    tag = f"gu{hh}_{w}"
                else:
                    ff, g3 = a, b
                    in_ap = _ap(wdT_d, ff * FF + g3 * 1024,
                                [[8 * FF, 128], [1, 1024]])
                    out_ap = _ap(scr_d, ff * FF + g3 * 1024,
                                 [[8 * FF, 128], [1, 1024]])
                    scr_res = f"scr_d{ff}_{g3 // 2}" if g3 < 2 else f"scr_d{ff}_1"
                    scr_res = None
                    tag = f"d{ff}_{g3}"
                S.add("sync", lambda s, in_ap=in_ap, slot=slot: s.dma_start(
                    out=stgi[slot], in_=in_ap),
                    writes=stgi_res[slot], dma=True, tag=f"dq_in_{tag}", chan=f"dqi{slot}")
                S.add("dve", lambda v, slot=slot: v.tensor_scalar(
                    out=stgb[slot], in0=stgi[slot], scalar1=QSCALE,
                    scalar2=None, op0=ALU.mult),
                    reads=stgi_res[slot], writes=stgb_res[slot])
                if kind == "gu":
                    wres = [f"scr_gu{a}"]
                else:
                    ff, g3 = a, b
                    wres = ([f"scr_d{ff}_0"] if g3 == 0 else
                            [f"scr_d{ff}_0", f"scr_d{ff}_1"] if g3 == 1 else
                            [f"scr_d{ff}_1"])
                S.add("gps", lambda g, out_ap=out_ap, slot=slot: g.dma_start(
                    out=out_ap, in_=stgb[slot]),
                    reads=stgb_res[slot], writes=wres,
                    dma=True, tag=f"dq_out_{tag}", chan=f"dqo{slot}")

        def emit_ln_p1(ms, hs, tag, fine=False):
            xs = x_sb[ms]
            xres = [f"x{ms}_{dd}" for dd in range(8)]
            for dd in range(8):
                S.add("dve", lambda v, dd=dd: v.tensor_copy(
                    out=u_sb[:, dd * TG:(dd + 1) * TG],
                    in_=xs[:, dd * TG:(dd + 1) * TG]),
                    reads=[xres[dd]], writes=[f"u{dd}"], tag=f"{tag}_cast{dd}")
                S.add("act", lambda sc, dd=dd: sc.activation(
                    out=ret_sb[:, dd * TG:(dd + 1) * TG],
                    in_=u_sb[:, dd * TG:(dd + 1) * TG], func=AF.Square),
                    reads=[f"u{dd}"],
                    writes=[f"ret{4 * dd + j}" for j in range(4)],
                    tag=f"{tag}_sq{dd}")
            for dd in range(8):
                S.add("pe", lambda t, dd=dd: t.matmul(
                    out=pb[4][0:1, :], lhsT=ones_c[:],
                    rhs=u_sb[:, dd * TG:(dd + 1) * TG],
                    start=(dd == 0), stop=(dd == 7)),
                    reads=["ones_c", f"u{dd}"], writes=["pb4"],
                    tag=f"{tag}_smm{dd}")
            for dd in range(8):
                S.add("pe", lambda t, dd=dd: t.matmul(
                    out=pb[2][0:1, :], lhsT=ones_c[:],
                    rhs=ret_sb[:, dd * TG:(dd + 1) * TG],
                    start=(dd == 0), stop=(dd == 7)),
                    reads=["ones_c"] + [f"ret{4 * dd + j}" for j in range(4)],
                    writes=["pb2"], tag=f"{tag}_qmm{dd}")
            if fine:
                yield
            S.add("dve", lambda v: v.tensor_copy(
                out=row(0), in_=pb[4][0:1, :]),
                reads=["pb4"], writes=["r_sa"], tag=f"{tag}_ev1")
            S.add("dve", lambda v: v.tensor_copy(
                out=row(1), in_=pb[2][0:1, :]),
                reads=["pb2"], writes=["r_sb"], tag=f"{tag}_ev2")
            S.add("dve", lambda v: v.tensor_scalar(
                out=rows[0:1, 1024:2048], in0=rows[0:1, 0:1024],
                scalar1=1.0 / D, scalar2=None, op0=ALU.mult),
                reads=["r_sa", "r_sb"], writes=["r_mu", "r_m2"],
                tag=f"{tag}_mu")
            S.add("dve", lambda v: v.tensor_tensor(
                out=row(1), in0=row(2), in1=row(2), op=ALU.mult),
                reads=["r_mu"], writes=["r_sb"], tag=f"{tag}_musq")
            S.add("dve", lambda v: v.scalar_tensor_tensor(
                out=row(3), in0=row(3), scalar=EPS, in1=row(1),
                op0=ALU.add, op1=ALU.subtract),
                reads=["r_m2", "r_sb"], writes=["r_var"], tag=f"{tag}_var")
            S.add("dve", lambda v: v.tensor_scalar(
                out=row(1, dt=I32), in0=row(3, dt=I32), scalar1=1,
                scalar2=None, op0=ALU.logical_shift_right),
                reads=["r_var"], writes=["r_sb"], tag=f"{tag}_sh")
            S.add("dve", lambda v: v.tensor_tensor(
                out=row(0, dt=I32), in0=cst_row[:], in1=row(1, dt=I32),
                op=ALU.subtract),
                reads=["cst_row", "r_sb"], writes=["r_y"], tag=f"{tag}_y0")
            for it in range(1):
                S.add("dve", lambda v: v.tensor_tensor(
                    out=row(1), in0=row(0), in1=row(0), op=ALU.mult),
                    reads=["r_y"], writes=["r_sb"], tag=f"{tag}_n{it}a")
                S.add("dve", lambda v: v.tensor_tensor(
                    out=row(1), in0=row(1), in1=row(3), op=ALU.mult),
                    reads=["r_sb", "r_var"], writes=["r_sb"], tag=f"{tag}_n{it}b")
                S.add("dve", lambda v: v.tensor_scalar(
                    out=row(1), in0=row(1), scalar1=-0.5, scalar2=1.5,
                    op0=ALU.mult, op1=ALU.add),
                    reads=["r_sb"], writes=["r_sb"], tag=f"{tag}_n{it}c")
                S.add("dve", lambda v: v.tensor_tensor(
                    out=row(0), in0=row(0), in1=row(1), op=ALU.mult),
                    reads=["r_y", "r_sb"], writes=["r_y"], tag=f"{tag}_n{it}d")
            if fine:
                yield

        def emit_ln_p2(ms, hs, tag, fine=False):
            xs = x_sb[ms]
            xres = [f"x{ms}_{dd}" for dd in range(8)]
            for c in range(N_CH):
                S.add("pe", lambda t, c=c: t.matmul(
                    out=pb[2][:, c * 128:(c + 1) * 128], lhsT=ones_r[:],
                    rhs=rows[0:1, c * 128:(c + 1) * 128],
                    start=True, stop=True),
                    reads=["ones_r", "r_y"], writes=["pb2"], tag=f"{tag}_bA{c}")
                S.add("pe", lambda t, c=c: t.matmul(
                    out=pb[3][:, c * 128:(c + 1) * 128], lhsT=ones_r[:],
                    rhs=rows[0:1, 1024 + c * 128:1024 + (c + 1) * 128],
                    start=True, stop=True),
                    reads=["ones_r", "r_mu"], writes=["pb3"], tag=f"{tag}_bB{c}")
            if fine:
                yield
            for dd in range(8):
                S.add("dve", lambda v, dd=dd: v.tensor_tensor(
                    out=ln_scr[:], in0=xs[:, dd * TG:(dd + 1) * TG],
                    in1=pb[3][:], op=ALU.subtract),
                    reads=[xres[dd], "pb3"], writes=["ln_scr"],
                    tag=f"{tag}_sub{dd}")
                S.add("dve", lambda v, dd=dd, hs=hs: v.tensor_tensor(
                    out=h_sb[hs][:, dd * TG:(dd + 1) * TG], in0=ln_scr[:],
                    in1=pb[2][:], op=ALU.mult),
                    reads=["ln_scr", "pb2"], writes=[f"h{hs}_{dd}"],
                    tag=f"{tag}_mul{dd}")
                if fine and dd in (1, 3, 5):
                    yield

        def dump(nm, src_ap, reads):
            if nm in dbg:
                S.add("gps", lambda g, nm=nm, src_ap=src_ap: g.dma_start(
                    out=dbg[nm].ap(), in_=src_ap), reads=reads, writes=[nm],
                    dma=True, tag=nm, chan="dbg")

        def xload(mm):
            mslot = mm % 2
            lb2, tg2 = mm % 2, mm // 2
            S.add("sync", lambda s, mslot=mslot, lb2=lb2, tg2=tg2: s.dma_start(
                out=x_sb[mslot][:],
                in_=_ap(x_t, (lb2 * (NMAC // 2) + tg2) * 4096,
                        [[B_CORE * (NMAC // 2) * 4096, 128], [1, 4096]])),
                writes=[f"x{mslot}_{dd}" for dd in range(8)], dma=True,
                tag=f"xload{mm}", chan=f"x{mslot}")

        def emit_mixer_parts(m):
            ms = m % 2
            hs = m % 2
            lb = m % 2
            xres = [f"x{ms}_{dd}" for dd in range(8)]
            hres = [f"h{hs}_{dd}" for dd in range(8)]
            hcur = h_sb[hs]

            xload(m)
            if m > 0:
                yield
            if m == 0:
                for _ in emit_ln_p1(ms, hs, f"ln1_{m}", fine=True):
                    pass
                for _ in emit_ln_p2(ms, hs, f"ln1_{m}", fine=True):
                    pass
                emit_startup_w12()
            else:
                yield from emit_ln_p1(ms, hs, f"ln1_{m}", fine=True)
                yield
                yield from emit_ln_p2(ms, hs, f"ln1_{m}", fine=True)
            if m == 0:
                dump("dbg_h1", hcur[:], hres)
            yield

            for h in range(H):
                bank = next_ret_bank()
                S.add("pe", lambda t, h=h, bank=bank: t.matmul(
                    out=pb[bank][:], lhsT=wq_sb[:, h * 128:(h + 1) * 128],
                    rhs=hcur[:, h * TG:(h + 1) * TG], start=True, stop=True),
                    reads=["wq", hres[h]], writes=[PBN[bank]], tag=f"qmm{m}_{h}")
                S.add("dve", lambda v, h=h, bank=bank: v.tensor_tensor(
                    out=_ap(qs_sb, h * TG, [[H * TG, 128], [128, 4], [1, 128]]),
                    in0=_ap(pb[bank], 0, [[512, 128], [128, 4], [1, 128]]),
                    in1=_ap(cgq, h * 128, [[D, 128], [0, 4], [1, 128]]),
                    op=ALU.mult),
                    reads=[PBN[bank], "cgq"], writes=[f"qs{h}"], tag=f"qev{m}_{h}")
                bank2 = next_ret_bank()
                S.add("pe", lambda t, h=h, bank2=bank2: t.matmul(
                    out=pb[bank2][:], lhsT=wk_sb[:, h * 128:(h + 1) * 128],
                    rhs=hcur[:, h * TG:(h + 1) * TG], start=True, stop=True),
                    reads=["wk", hres[h]], writes=[PBN[bank2]], tag=f"kmm{m}_{h}")
                S.add("dve", lambda v, h=h, bank2=bank2: v.tensor_tensor(
                    out=_ap(ks_sb, h * TG, [[H * TG, 128], [128, 4], [1, 128]]),
                    in0=_ap(pb[bank2], 0, [[512, 128], [128, 4], [1, 128]]),
                    in1=_ap(cgk, h * 128, [[D, 128], [0, 4], [1, 128]]),
                    op=ALU.mult),
                    reads=[PBN[bank2], "cgk"], writes=[f"ks{h}"], tag=f"kev{m}_{h}")
                if h == 3:
                    yield
            yield

            for gg in range(8):
                bank = next_ret_bank()
                for dd in range(8):
                    S.add("pe", lambda t, gg=gg, dd=dd, bank=bank: t.matmul(
                        out=pb[bank][:],
                        lhsT=w1_sb[:, dd * 1024 + gg * 128:dd * 1024 + (gg + 1) * 128],
                        rhs=hcur[:, dd * TG:(dd + 1) * TG],
                        start=(dd == 0), stop=(dd == 7)),
                        reads=[f"w1_{dd}", hres[dd]], writes=[PBN[bank]],
                        tag=f"w1mm{m}_{gg}_{dd}")
                S.add("act", lambda sc, gg=gg, bank=bank: sc.activation(
                    out=u_sb[:, gg * TG:(gg + 1) * TG], in_=pb[bank][:],
                    func=AF.Silu),
                    reads=[PBN[bank]], writes=[f"u{gg}"], tag=f"w1silu{m}_{gg}")
                S.add("dve", lambda v, gg=gg: v.tensor_scalar(
                    out=u_sb[:, gg * TG:(gg + 1) * TG],
                    in0=u_sb[:, gg * TG:(gg + 1) * TG],
                    scalar1=gnb[:, gg:gg + 1], scalar2=None, op0=ALU.add),
                    reads=[f"u{gg}", "gnb"], writes=[f"u{gg}"], tag=f"gnb{m}_{gg}")
                if gg == 3:
                    yield
            if m == 0:
                dump("dbg_qs", qs_sb[:], [f"qs{h}" for h in range(H)])
                dump("dbg_ks", ks_sb[:], [f"ks{h}" for h in range(H)])
            yield

            for c in range(N_CH):
                for h in range(H):
                    bk = next_ret_bank()
                    S.add("pe", lambda t, h=h, c=c, bk=bk: t.matmul(
                        out=pb[bk][:, 0:128],
                        lhsT=hcur[:, h * TG + c * 128:h * TG + (c + 1) * 128],
                        rhs=wk_sb[:, h * 128:(h + 1) * 128],
                        start=True, stop=True),
                        reads=[hres[h], "wk"], writes=[PBN[bk]],
                        tag=f"ktm{m}_{c}_{h}")
                    S.add("act", lambda sc, h=h, bk=bk: sc.activation(
                        out=ktm[:, h * 128:(h + 1) * 128], in_=pb[bk][:, 0:128],
                        func=AF.Identity, scale=cgd[:, h:h + 1]),
                        reads=[PBN[bk], "cgd"], writes=[f"ktm{h}"],
                        tag=f"ktmev{m}_{c}_{h}")
                    bv = next_ret_bank()
                    S.add("pe", lambda t, h=h, c=c, bv=bv: t.matmul(
                        out=pb[bv][:, 0:128],
                        lhsT=hcur[:, h * TG + c * 128:h * TG + (c + 1) * 128],
                        rhs=wv_sb[:, h * 128:(h + 1) * 128],
                        start=True, stop=True),
                        reads=[hres[h], "wv"], writes=[PBN[bv]],
                        tag=f"vtm{m}_{c}_{h}")
                    S.add("act", lambda sc, h=h, bv=bv: sc.activation(
                        out=vtm[:, h * 128:(h + 1) * 128], in_=pb[bv][:, 0:128],
                        func=AF.Copy),
                        reads=[PBN[bv]], writes=[f"vtm{h}"],
                        tag=f"vtmev{m}_{c}_{h}")
                    bs = next_ret_bank()
                    S.add("pe", lambda t, h=h, c=c, bs=bs: t.matmul(
                        out=pb[bs][:, 0:128],
                        lhsT=ks_sb[:, h * TG + c * 128:h * TG + (c + 1) * 128],
                        rhs=qs_sb[:, h * TG + c * 128:h * TG + (c + 1) * 128],
                        start=True, stop=True),
                        reads=[f"ks{h}", f"qs{h}"], writes=[PBN[bs]],
                        tag=f"sc{m}_{c}_{h}")
                    S.add("dve", lambda v, h=h, bs=bs: v.tensor_tensor(
                        out=sct[:, h * 128:(h + 1) * 128], in0=pb[bs][:, 0:128],
                        in1=causal[:], op=ALU.mult),
                        reads=[PBN[bs], "causal"], writes=[f"sct{h}"],
                        tag=f"scev{m}_{c}_{h}")
                yield
                for h in range(H):
                    sres = f"S{lb}_{h}"
                    sbres = f"Sb{lb}_{h}"
                    ridx = c * 8 + h
                    br = next_ret_bank()
                    S.add("pe", lambda t, h=h, br=br: t.matmul(
                        out=pb[br][:, 0:128], lhsT=sct[:, h * 128:(h + 1) * 128],
                        rhs=vtm[:, h * 128:(h + 1) * 128], start=True, stop=False),
                        reads=[f"sct{h}", f"vtm{h}"], writes=[PBN[br]],
                        tag=f"ri{m}_{c}_{h}")
                    S.add("pe", lambda t, h=h, c=c, br=br, lb=lb: t.matmul(
                        out=pb[br][:, 0:128],
                        lhsT=qs_sb[:, h * TG + c * 128:h * TG + (c + 1) * 128],
                        rhs=S_b[:, (lb * 8 + h) * 128:(lb * 8 + h + 1) * 128],
                        start=False, stop=True),
                        reads=[f"qs{h}", sbres], writes=[PBN[br]],
                        tag=f"rx{m}_{c}_{h}")
                    S.add("dve", lambda v, ridx=ridx, br=br: v.tensor_scalar(
                        out=ret_sb[:, ridx * 128:(ridx + 1) * 128],
                        in0=pb[br][:, 0:128], scalar1=1.0, scalar2=None,
                        op0=ALU.mult, op1=ALU.add,
                        accum_out=ssum[:, ridx:ridx + 1]),
                        reads=[PBN[br]], writes=[f"ret{ridx}", "ssum"],
                        tag=f"rev{m}_{c}_{h}")
                    S.add("dve", lambda v, ridx=ridx: v.scalar_tensor_tensor(
                        out=sq_sb[:, (ridx % 4) * 128:(ridx % 4 + 1) * 128],
                        in0=ret_sb[:, ridx * 128:(ridx + 1) * 128], scalar=1.0,
                        in1=ret_sb[:, ridx * 128:(ridx + 1) * 128],
                        op0=ALU.mult, op1=ALU.mult,
                        accum_out=ssq[:, ridx:ridx + 1]),
                        reads=[f"ret{ridx}"], writes=[f"sqs{ridx % 4}", "ssq"],
                        tag=f"rsq{m}_{c}_{h}")
                    bkv = next_ret_bank()
                    S.add("pe", lambda t, h=h, bkv=bkv: t.matmul(
                        out=pb[bkv][:, 0:128], lhsT=ktm[:, h * 128:(h + 1) * 128],
                        rhs=vtm[:, h * 128:(h + 1) * 128], start=True, stop=True),
                        reads=[f"ktm{h}", f"vtm{h}"], writes=[PBN[bkv]],
                        tag=f"kv{m}_{c}_{h}")
                    gC = float(GAMMAS[h] ** C)
                    S.add("dve", lambda v, h=h, bkv=bkv, gC=gC, lb=lb:
                          v.scalar_tensor_tensor(
                        out=S_f[:, (lb * 8 + h) * 128:(lb * 8 + h + 1) * 128],
                        in0=S_f[:, (lb * 8 + h) * 128:(lb * 8 + h + 1) * 128],
                        scalar=gC, in1=pb[bkv][:, 0:128],
                        op0=ALU.mult, op1=ALU.add),
                        reads=[sres, PBN[bkv]], writes=[sres],
                        tag=f"st{m}_{c}_{h}")
                    S.add("act", lambda sc, h=h, lb=lb: sc.activation(
                        out=S_b[:, (lb * 8 + h) * 128:(lb * 8 + h + 1) * 128],
                        in_=S_f[:, (lb * 8 + h) * 128:(lb * 8 + h + 1) * 128],
                        func=AF.Copy),
                        reads=[sres], writes=[sbres], tag=f"stb{m}_{c}_{h}")
                    if m == 0 and h % 2 == 0:
                        emit_dequant(1)
                if m == 0 and c == 0:
                    dump("dbg_ktm", ktm[:], [f"ktm{h}" for h in range(H)])
                    dump("dbg_vtm", vtm[:], [f"vtm{h}" for h in range(H)])
                    dump("dbg_sct", sct[:], [f"sct{h}" for h in range(H)])
                yield

            if m == 0:
                dump("dbg_ret", ret_sb[:], [f"ret{i}" for i in range(32)])
                dump("dbg_S", S_f[:], [f"S{0}_{h}" for h in range(H)])
                dump("dbg_ssum", ssum[:], ["ssum"])

            S.add("dve", lambda v: v.tensor_scalar(
                out=mu_g[:], in0=ssum[:], scalar1=1.0 / HD, scalar2=None,
                op0=ALU.mult), reads=["ssum"], writes=["mu_g"], tag=f"g1_{m}")
            S.add("dve", lambda v: v.tensor_scalar(
                out=tmp_g[:], in0=ssq[:], scalar1=1.0 / HD, scalar2=None,
                op0=ALU.mult), reads=["ssq"], writes=["tmp_g"], tag=f"g2_{m}")
            S.add("dve", lambda v: v.tensor_tensor(
                out=rstd_g[:], in0=mu_g[:], in1=mu_g[:], op=ALU.mult),
                reads=["mu_g"], writes=["rstd_g"], tag=f"g3_{m}")
            S.add("dve", lambda v: v.scalar_tensor_tensor(
                out=varpe_g[:], in0=tmp_g[:], scalar=EPS, in1=rstd_g[:],
                op0=ALU.add, op1=ALU.subtract),
                reads=["tmp_g", "rstd_g"], writes=["varpe_g"], tag=f"g4_{m}")
            S.add("dve", lambda v: v.tensor_scalar(
                out=tmp_g[:].bitcast(I32), in0=varpe_g[:].bitcast(I32),
                scalar1=1, scalar2=None, op0=ALU.logical_shift_right),
                reads=["varpe_g"], writes=["tmp_g"], tag=f"g5_{m}")
            S.add("dve", lambda v: v.tensor_tensor(
                out=rstd_g[:].bitcast(I32), in0=cstC[:], in1=tmp_g[:].bitcast(I32),
                op=ALU.subtract),
                reads=["cstC", "tmp_g"], writes=["rstd_g"], tag=f"g6_{m}")
            for it in range(2):
                S.add("dve", lambda v: v.tensor_tensor(
                    out=tmp_g[:], in0=rstd_g[:], in1=rstd_g[:], op=ALU.mult),
                    reads=["rstd_g"], writes=["tmp_g"], tag=f"g7{it}_{m}")
                S.add("dve", lambda v: v.tensor_tensor(
                    out=tmp_g[:], in0=tmp_g[:], in1=varpe_g[:], op=ALU.mult),
                    reads=["tmp_g", "varpe_g"], writes=["tmp_g"], tag=f"g8{it}_{m}")
                S.add("dve", lambda v: v.tensor_scalar(
                    out=tmp_g[:], in0=tmp_g[:], scalar1=-0.5, scalar2=1.5,
                    op0=ALU.mult, op1=ALU.add),
                    reads=["tmp_g"], writes=["tmp_g"], tag=f"g9{it}_{m}")
                S.add("dve", lambda v: v.tensor_tensor(
                    out=rstd_g[:], in0=rstd_g[:], in1=tmp_g[:], op=ALU.mult),
                    reads=["rstd_g", "tmp_g"], writes=["rstd_g"], tag=f"gA{it}_{m}")
            S.add("dve", lambda v: v.scalar_tensor_tensor(
                out=bcol_g[:], in0=mu_g[:], scalar=-1.0, in1=rstd_g[:],
                op0=ALU.mult, op1=ALU.mult),
                reads=["mu_g", "rstd_g"], writes=["bcol_g"], tag=f"gB_{m}")
            yield
            for ridx in range(32):
                c, h = ridx // 8, ridx % 8
                rs = ridx % 4
                S.add("act", lambda sc, ridx=ridx, rs=rs: sc.activation(
                    out=retn[:, rs * 128:(rs + 1) * 128],
                    in_=ret_sb[:, ridx * 128:(ridx + 1) * 128],
                    func=AF.Identity, scale=rstd_g[:, ridx:ridx + 1],
                    bias=bcol_g[:, ridx:ridx + 1]),
                    reads=[f"ret{ridx}", "rstd_g", "bcol_g"],
                    writes=[f"retn{rs}"], tag=f"ap{m}_{ridx}")
                tb = 2 + (ridx % 2)
                S.add("pe", lambda t, rs=rs, tb=tb: t.transpose(
                    out=pb[tb][:].bitcast(BF16)[:, 0:128],
                    in_=retn[:, rs * 128:(rs + 1) * 128], identity=ident[:]),
                    reads=[f"retn{rs}", "ident"], writes=[PBN[tb]],
                    tag=f"tr{m}_{ridx}")
                S.add("dve", lambda v, h=h, c=c, tb=tb: v.scalar_tensor_tensor(
                    out=u_sb[:, h * TG + c * 128:h * TG + (c + 1) * 128],
                    in0=pb[tb][:].bitcast(BF16)[:, 0:128],
                    scalar=gnw[:, h:h + 1],
                    in1=u_sb[:, h * TG + c * 128:h * TG + (c + 1) * 128],
                    op0=ALU.mult, op1=ALU.add),
                    reads=[PBN[tb], "gnw", f"u{h}"], writes=[f"u{h}"],
                    tag=f"ua{m}_{ridx}")
                if ridx == 15:
                    yield
            if m == 0:
                dump("dbg_rstd", rstd_g[:], ["rstd_g"])
                dump("dbg_u", u_sb[:], [f"u{gg}" for gg in range(8)])
            yield

            for ff in range(8):
                bank = next_ret_bank()
                for gg in range(8):
                    S.add("pe", lambda t, ff=ff, gg=gg, bank=bank: t.matmul(
                        out=pb[bank][:],
                        lhsT=w2_sb[:, gg * 1024 + ff * 128:gg * 1024 + (ff + 1) * 128],
                        rhs=u_sb[:, gg * TG:(gg + 1) * TG],
                        start=(gg == 0), stop=(gg == 7)),
                        reads=[f"w2_{gg}", f"u{gg}"], writes=[PBN[bank]],
                        tag=f"w2mm{m}_{ff}_{gg}")
                S.add("dve", lambda v, ff=ff, bank=bank, ms=ms:
                      v.scalar_tensor_tensor(
                    out=x_sb[ms][:, ff * TG:(ff + 1) * TG],
                    in0=x_sb[ms][:, ff * TG:(ff + 1) * TG], scalar=1.0,
                    in1=pb[bank][:], op0=ALU.mult, op1=ALU.add),
                    reads=[xres[ff], PBN[bank]], writes=[xres[ff]],
                    tag=f"w2ev{m}_{ff}")
                if ff == 3:
                    yield
            if m == 0:
                dump("dbg_x1", x_sb[0][:], xres)

        def emit_ln2_parts(m):
            ms = m % 2
            hs = m % 2
            yield from emit_ln_p1(ms, hs, f"ln2_{m}", fine=True)
            yield
            yield from emit_ln_p2(ms, hs, f"ln2_{m}", fine=True)
            if m == 0:
                dump("dbg_h2", h_sb[hs][:], [f"h{hs}_{dd}" for dd in range(8)])

        def gu_dma(m, hh, slot):
            S.add("sync", lambda s, hh=hh, slot=slot: s.dma_start(
                out=wgu_sb[slot][:],
                in_=_ap(scr_gu, hh * 2048, [[NHH * 2048, 128], [1, 2048]])),
                reads=[f"scr_gu{hh}"], writes=[f"wgu{slot}"], dma=True,
                tag=f"lgu{m}_{hh}", chan=f"wgu{slot}")

        def gu_mm(m, hcur, hres, hh, dd, slot, which, bank):
            off = 1024 if which else 0
            S.add("pe", lambda t, hh=hh, dd=dd, slot=slot, off=off,
                  bank=bank, hcur=hcur: t.matmul(
                out=pb[bank][:],
                lhsT=wgu_sb[slot][:, off + dd * 128:off + (dd + 1) * 128],
                rhs=hcur[:, dd * TG:(dd + 1) * TG],
                start=(dd == 0), stop=(dd == 7)),
                reads=[f"wgu{slot}", hres[dd]], writes=[PBN[bank]],
                tag=f"{'u' if which else 'g'}mm{m}_{hh}_{dd}")

        def gu_tail(m, hh, slot, gbank, ubank):
            S.add("act", lambda sc, slot=slot, gbank=gbank: sc.activation(
                out=sg_sb[slot][:], in_=pb[gbank][:], func=AF.Silu),
                reads=[PBN[gbank]], writes=[f"sg{slot}"],
                tag=f"gsilu{m}_{hh}")
            S.add("dve", lambda v, hh=hh, slot=slot, ubank=ubank:
                  v.tensor_tensor(
                out=hid[:, hh * TG:(hh + 1) * TG], in0=pb[ubank][:],
                in1=sg_sb[slot][:], op=ALU.mult),
                reads=[PBN[ubank], f"sg{slot}"], writes=[f"hid{hh}"],
                tag=f"hid{m}_{hh}")

        def emit_ffn_and_next(m, nxt_gen):
            ms = m % 2
            hs = m % 2
            lb = m % 2
            tgi = m // 2
            xres = [f"x{ms}_{dd}" for dd in range(8)]
            hres = [f"h{hs}_{dd}" for dd in range(8)]
            hcur = h_sb[hs]

            def pump():
                if nxt_gen is not None:
                    try:
                        next(nxt_gen)
                    except StopIteration:
                        pass

            for hh in range(NHH):
                if m == 0 and hh < 12:
                    emit_dequant(5)
                if hh >= 2:
                    pump()
                slot = hh % 2
                gu_dma(m, hh, slot)
                for dd in range(8):
                    gu_mm(m, hcur, hres, hh, dd, slot, 0, 0)
                for dd in range(8):
                    gu_mm(m, hcur, hres, hh, dd, slot, 1, 1)
                gu_tail(m, hh, slot, 0, 1)
            if nxt_gen is not None:
                for _ in nxt_gen:
                    pass
            if m == 0:
                dump("dbg_hid", hid[:], [f"hid{t}" for t in range(NHH)])

            l2 = emit_ln2_parts(m + 1) if m + 1 < NMAC else None

            def pump2():
                if l2 is not None:
                    try:
                        next(l2)
                    except StopIteration:
                        pass

            for ff in range(8):
                pump2()
                bank = ff % 2
                for g2 in range(2):
                    slot = (ff * 2 + g2) % 2
                    S.add("sync", lambda s, ff=ff, g2=g2, slot=slot: s.dma_start(
                        out=wd_sb[slot][:],
                        in_=_ap(scr_d, ff * FF + g2 * 1536,
                                [[8 * FF, 128], [1, 1536]])),
                        reads=[f"scr_d{ff}_{g2}"], writes=[f"wds{slot}"],
                        dma=True, tag=f"lwd{m}_{ff}_{g2}", chan=f"wd{slot}")
                    for hh in range(g2 * 12, g2 * 12 + 12):
                        S.add("pe", lambda t, ff=ff, hh=hh, g2=g2, slot=slot,
                              bank=bank: t.matmul(
                            out=pb[bank][:],
                            lhsT=wd_sb[slot][:, (hh - g2 * 12) * 128:
                                             (hh - g2 * 12 + 1) * 128],
                            rhs=hid[:, hh * TG:(hh + 1) * TG],
                            start=(hh == 0), stop=(hh == NHH - 1)),
                            reads=[f"wds{slot}", f"hid{hh}"],
                            writes=[PBN[bank]], tag=f"dmm{m}_{ff}_{hh}")
                S.add("dve", lambda v, ff=ff, bank=bank, ms=ms: v.tensor_tensor(
                    out=x_sb[ms][:, ff * TG:(ff + 1) * TG],
                    in0=x_sb[ms][:, ff * TG:(ff + 1) * TG],
                    in1=pb[bank][:], op=ALU.add),
                    reads=[xres[ff], PBN[bank]], writes=[xres[ff]],
                    tag=f"dev{m}_{ff}")
            if l2 is not None:
                for _ in l2:
                    pass

            S.add("gps", lambda g, ms=ms, lb=lb, tgi=tgi: g.dma_start(
                out=_ap(y_t, (lb * (NMAC // 2) + tgi) * 4096,
                        [[B_CORE * (NMAC // 2) * 4096, 128], [1, 4096]]),
                in_=x_sb[ms][:]),
                reads=xres, writes=[f"yout{m}"], dma=True, tag=f"store{m}",
                chan=f"st{ms}")

        emit_startup()
        for _ in emit_mixer_parts(0):
            pass
        for _ in emit_ln2_parts(0):
            pass
        for m in range(NMAC):
            nxt = emit_mixer_parts(m + 1) if m + 1 < NMAC else None
            emit_ffn_and_next(m, nxt)
        emit_dequant(1000)

        S.schedule()

        sems.update({k: e(nc.semaphore(f"s_{k}"))
                     for k in S.keys if k not in sems})

        block = e(nc.Block())

        def replay(eng_name):
            def go(eng):
                hw = {}
                for op in S.per_eng[eng_name]:
                    for en, v in sorted(op["waits"].items()):
                        if hw.get(en, 0) < v:
                            eng.wait_ge(sems[en], v)
                            hw[en] = v
                    inst = op["fn"](eng)
                    if op["inc"]:
                        inst.then_inc(sems[op["key"]], op["amount"])
                if eng_name == "gps":
                    for k in ("st0", "st1"):
                        if k in S.final_counts:
                            eng.wait_ge(sems[k], S.final_counts[k])
            return go

        block.tensor(replay("pe"))
        block.scalar(replay("act"))
        block.vector(replay("dve"))
        block.gpsimd(replay("gps"))
        block.sync(replay("sync"))

    return nc


_NC_CACHE = []
_LAST = {}


def _get_nc():
    if not _NC_CACHE:
        _NC_CACHE.append(build_program())
    return _NC_CACHE[0]


def _host_constants():
    g = np.array(GAMMAS, dtype=np.float64)
    t = np.arange(128, dtype=np.float64)
    cgq = np.empty((128, D), dtype=np.float32)
    cgk = np.empty((128, D), dtype=np.float32)
    for h in range(H):
        cgq[:, h * 128:(h + 1) * 128] = (g[h] ** (t + 1.0)).astype(np.float32)[None, :]
        cgk[:, h * 128:(h + 1) * 128] = (g[h] ** (-(t + 1.0))).astype(np.float32)[None, :]
    cgd = np.empty((128, H), dtype=np.float32)
    for h in range(H):
        cgd[:, h] = (g[h] ** (127.0 - t)).astype(np.float32)
    return cgq, cgk, cgd


def kernel(L0_attn_x, L0_attn_Wq, L0_attn_Wk, L0_attn_Wv, L0_attn_W1,
           L0_attn_W2, L0_attn_gn_w, L0_attn_gn_b, L0_ffn_w_gate,
           L0_ffn_w_up, L0_ffn_w_down):
    x = np.asarray(L0_attn_x, dtype=np.float32)
    wq = np.ascontiguousarray(
        np.asarray(L0_attn_Wq, dtype=np.float32).transpose(1, 0, 2)
        .reshape(128, H * HD))
    wk = np.ascontiguousarray(
        np.asarray(L0_attn_Wk, dtype=np.float32).transpose(1, 0, 2)
        .reshape(128, H * HD))
    wv = np.ascontiguousarray(
        np.asarray(L0_attn_Wv, dtype=np.float32).transpose(1, 0, 2)
        .reshape(128, H * HD))
    w1 = np.ascontiguousarray(np.asarray(L0_attn_W1, dtype=np.float32))
    w2 = np.ascontiguousarray(np.asarray(L0_attn_W2, dtype=np.float32))
    gnwT = np.ascontiguousarray(
        np.asarray(L0_attn_gn_w, dtype=np.float32).reshape(H, 128).T)
    gnbT = np.ascontiguousarray(
        np.asarray(L0_attn_gn_b, dtype=np.float32).reshape(H, 128).T)
    wgT = np.ascontiguousarray(
        np.asarray(L0_ffn_w_gate, dtype=np.int32)
        .reshape(NHH, 128, 8, 128).transpose(3, 0, 2, 1)
        .reshape(128, NHH * 1024))
    wuT = np.ascontiguousarray(
        np.asarray(L0_ffn_w_up, dtype=np.int32)
        .reshape(NHH, 128, 8, 128).transpose(3, 0, 2, 1)
        .reshape(128, NHH * 1024))
    wdT = np.ascontiguousarray(
        np.asarray(L0_ffn_w_down, dtype=np.int32)
        .reshape(8, 128, NHH, 128).transpose(3, 0, 2, 1)
        .reshape(128, 8 * FF))
    cgq, cgk, cgd = _host_constants()

    nc = _get_nc()
    in_maps = []
    for ci in range(N_CORES):
        xs = np.ascontiguousarray(
            x[ci * B_CORE:(ci + 1) * B_CORE]
            .reshape(B_CORE, NMAC // 2, TG, 8, 128)
            .transpose(4, 0, 1, 3, 2))
        in_maps.append({
            "x_t": xs, "wq": wq, "wk": wk, "wv": wv, "w1": w1, "w2": w2,
            "gnwT": gnwT, "gnbT": gnbT, "wgT": wgT, "wuT": wuT, "wdT": wdT,
            "cgq": cgq, "cgk": cgk, "cgd": cgd,
        })
    res = run_bass_kernel_spmd(nc, in_maps, core_ids=list(range(N_CORES)))
    _LAST.clear()
    _LAST["res"] = res
    out = np.empty((16, SEQ, D), dtype=np.float32)
    for ci in range(N_CORES):
        yt = res.results[ci]["y_t"]
        out[ci * B_CORE:(ci + 1) * B_CORE] = (
            yt.transpose(1, 2, 4, 3, 0).reshape(B_CORE, SEQ, D))
    return out


if __name__ == "__main__":
    import json
    nc = build_program()
    print("built ok, instructions:", nc.n_instructions())
